# version 15
# baseline (speedup 1.0000x reference)
import numpy as np
from contextlib import ExitStack

import concourse.bass as bass
import concourse.mybir as mybir
from concourse.bass_utils import run_bass_kernel_spmd

F32 = mybir.dt.float32
BF16 = mybir.dt.bfloat16
ALU = mybir.AluOpType
AFT = mybir.ActivationFunctionType
AX = mybir.AxisListType.X

NCORE = 8
D = 1024
TOK_CORE = 2048
T_PASS = 256
N_PASS = TOK_CORE // T_PASS
N_TILE = T_PASS // 128
N_GRP = T_PASS // 8

ENGS = ("sync", "gpsimd", "tensor", "vector", "scalar")
SEMS = ("w", "g", "pe", "ve", "se", "gp", "o")
ESEM = {"sync": "w", "gpsimd": "gp", "tensor": "pe", "vector": "ve", "scalar": "se"}


class Prog:
    def __init__(self):
        self.ops = {e: [] for e in ENGS}
        self.cnt = {s: 0 for s in SEMS}
        self.lastw = {e: {s: 0 for s in SEMS} for e in ENGS}

    DMA_SEMS = ("w", "g", "o")

    def emit(self, eng, fn=None, waits=(), inc=None):
        ws = []
        for s, vv in waits:
            if s in self.DMA_SEMS:
                vv = self.cnt[s]
            if vv > self.lastw[eng][s]:
                ws.append((s, vv))
                self.lastw[eng][s] = vv
        if inc is not None:
            self.cnt[inc[0]] += inc[1]
        self.ops[eng].append((fn, ws, inc))

    def now(self, s):
        return self.cnt[s]


def build():
    nc = bass.Bass("TRN2", target_bir_lowering=False, detect_race_conditions=False)

    x_d = nc.dram_tensor("x", [TOK_CORE, D], F32, kind="ExternalInput").ap()
    out_d = nc.dram_tensor("out", [TOK_CORE, D], F32, kind="ExternalOutput").ap()
    wq_d = nc.dram_tensor("wq", [D, D], F32, kind="ExternalInput").ap()
    wk_d = nc.dram_tensor("wk", [D, D], F32, kind="ExternalInput").ap()
    wv_d = nc.dram_tensor("wv", [D, D], F32, kind="ExternalInput").ap()
    wo_d = nc.dram_tensor("wo", [D, D], F32, kind="ExternalInput").ap()
    fcw_d = nc.dram_tensor("fcw", [D, 4096], F32, kind="ExternalInput").ap()
    pjw_d = nc.dram_tensor("pjw", [4096, D], F32, kind="ExternalInput").ap()
    wgT_d = nc.dram_tensor("wgT", [D, 4096], F32, kind="ExternalInput").ap()
    wuT_d = nc.dram_tensor("wuT", [D, 4096], F32, kind="ExternalInput").ap()
    wdT_d = nc.dram_tensor("wdT", [4096, D], F32, kind="ExternalInput").ap()
    ident_d = nc.dram_tensor("ident", [128, 128], F32, kind="ExternalInput").ap()
    mask_d = nc.dram_tensor("mask", [128, 128], F32, kind="ExternalInput").ap()
    brows_d = nc.dram_tensor("brows", [65, D + 128], F32, kind="ExternalInput").ap()
    bqc_d = nc.dram_tensor("bqc", [128, 8], F32, kind="ExternalInput").ap()
    bkc_d = nc.dram_tensor("bkc", [128, 8], F32, kind="ExternalInput").ap()
    g1c_d = nc.dram_tensor("g1c", [128, 8], F32, kind="ExternalInput").ap()
    b1c_d = nc.dram_tensor("b1c", [128, 8], F32, kind="ExternalInput").ap()
    g2c_d = nc.dram_tensor("g2c", [128, 8], F32, kind="ExternalInput").ap()
    b2c_d = nc.dram_tensor("b2c", [128, 8], F32, kind="ExternalInput").ap()
    fcbc_d = nc.dram_tensor("fcbc", [128, 32], F32, kind="ExternalInput").ap()
    qg_d = nc.dram_tensor("qg_s", [D, 4096], BF16).ap()
    qu_d = nc.dram_tensor("qu_s", [D, 4096], BF16).ap()
    qd_d = nc.dram_tensor("qd_s", [4096, D], BF16).ap()

    wAB = nc.alloc_sbuf_tensor("wAB", [128, 16 * 1024], F32)
    big = nc.alloc_sbuf_tensor("big", [128, 8192], F32)
    k2T = nc.alloc_sbuf_tensor("k2T", [128, 8192], BF16)
    Aq = nc.alloc_sbuf_tensor("Aq", [64, 4096], F32)
    Ak = nc.alloc_sbuf_tensor("Ak", [64, 4096], F32)
    scrA = nc.alloc_sbuf_tensor("scrA", [128, 2048], F32)
    scrB = nc.alloc_sbuf_tensor("scrB", [128, 2048], F32)
    attnEO = nc.virtual_sbuf_tensor("attnEO", [128, 2048], F32)
    nc.back_virtual_tensor_with(attnEO, scrA)
    u_a = nc.alloc_sbuf_tensor("u_a", [128, 2048], F32)
    v_tm = nc.alloc_sbuf_tensor("v_tm", [128, 2048], F32)
    x1_a = nc.alloc_sbuf_tensor("x1_a", [128, 2048], F32)
    actT = nc.alloc_sbuf_tensor("actT", [128, 2048], F32)
    kxT = nc.alloc_sbuf_tensor("kxT", [128, 2048], BF16)
    scrC = nc.alloc_sbuf_tensor("scrC", [128, 1024], F32)
    qdb = nc.alloc_sbuf_tensor("qdb", [128, 512], BF16)
    Vg = nc.alloc_sbuf_tensor("Vg", [128, 132], F32)
    stat = nc.alloc_sbuf_tensor("stat", [128, 64], F32)
    wsc = nc.alloc_sbuf_tensor("wsc", [128, 8], F32)
    onecol = nc.alloc_sbuf_tensor("onecol", [128, 1], F32)
    ident = nc.alloc_sbuf_tensor("idents", [128, 128], F32)
    maskc = nc.alloc_sbuf_tensor("maskc", [128, 128], F32)
    brows = nc.alloc_sbuf_tensor("brows_s", [65, D + 128], F32)
    bqc = nc.alloc_sbuf_tensor("bqcs", [128, 8], F32)
    bkc = nc.alloc_sbuf_tensor("bkcs", [128, 8], F32)
    g1c = nc.alloc_sbuf_tensor("g1cs", [128, 8], F32)
    b1c = nc.alloc_sbuf_tensor("b1cs", [128, 8], F32)
    g2c = nc.alloc_sbuf_tensor("g2cs", [128, 8], F32)
    b2c = nc.alloc_sbuf_tensor("b2cs", [128, 8], F32)
    fcbc = nc.alloc_sbuf_tensor("fcbcs", [128, 32], F32)

    scr0 = scrA[:, 0:1024]
    scr1 = scrA[:, 1024:2048]
    scr2 = scrB[:, 0:1024]
    scr3 = scrB[:, 1024:2048]
    scrCv = scrC[:, 0:1024]
    agb = big[:, 1280:1536]
    Smb = big[:, 1536:1664]
    fmq = big[:, 0:256]
    fmk = big[:, 256:512]
    elu0 = big[:, 512:768]
    elu1 = big[:, 768:1024]
    elu2 = big[:, 1024:1280]

    ps = [nc.alloc_psum_tensor("ps%d" % i, [128, 512], F32) for i in range(8)]

    P = Prog()

    def E(eng, fn, waits=()):
        sem = ESEM[eng]
        P.emit(eng, fn, waits, (sem, 1))
        return P.now(sem)

    def DMA(eng, out_ap, in_ap, waits=(), osem=None):
        sem = osem or ("w" if eng == "sync" else "g")
        P.emit(eng, lambda e, o=out_ap, i=in_ap: e.dma_start(out=o, in_=i), waits, (sem, 16))
        return P.now(sem)

    def FENCE(eng):
        sem = ESEM[eng]
        P.emit(eng, None, [(sem, P.now(sem))])

    def GUARD(eng, *sems):
        P.emit(eng, None, [(s, P.now(s)) for s in sems])

    def Wn(*sems):
        return [(s, P.now(s)) for s in sems]

    Aq_v = Aq.ap().rearrange("p (g t h) -> p g t h", g=N_GRP, t=8, h=16)
    Ak_v = Ak.ap().rearrange("p (g t h) -> p g t h", g=N_GRP, t=8, h=16)
    eo_v = attnEO.ap().rearrange("p (g t c) -> p g t c", g=N_GRP, t=8, c=8)

    for dst, src in ((ident, ident_d), (maskc, mask_d), (brows, brows_d), (bqc, bqc_d), (bkc, bkc_d),
                     (g1c, g1c_d), (b1c, b1c_d), (g2c, g2c_d), (b2c, b2c_d), (fcbc, fcbc_d)):
        cw = DMA("sync", dst[:], src)
    E("gpsimd", lambda e: e.memset(onecol[:], 1.0))
    E("gpsimd", lambda e: e.memset(Vg[:, 64:65], 1.0))
    init_g = E("gpsimd", lambda e: e.memset(Vg[:, 130:131], 1.0))

    def emit_ln(src_ap, dbuf_ap, sq_ap, F=D):
        E("vector", lambda e: e.reduce_sum(out=stat[:, 0:1], in_=src_ap, axis=AX))
        FENCE("vector")
        E("vector", lambda e: e.tensor_single_scalar(out=stat[:, 1:2], in_=stat[:, 0:1],
                                                     scalar=1.0 / F, op=ALU.mult))
        FENCE("vector")
        E("vector", lambda e: e.tensor_scalar(out=dbuf_ap, in0=src_ap, scalar1=stat[:, 1:2],
                                              scalar2=None, op0=ALU.subtract))
        E("scalar", lambda e: e.activation(out=sq_ap, in_=dbuf_ap, func=AFT.Square), Wn("ve"))
        E("vector", lambda e: e.reduce_sum(out=stat[:, 2:3], in_=sq_ap, axis=AX), Wn("se"))
        FENCE("vector")
        E("vector", lambda e: e.tensor_scalar(out=stat[:, 3:4], in0=stat[:, 2:3], scalar1=1.0 / F,
                                              scalar2=1e-5, op0=ALU.mult, op1=ALU.add))
        FENCE("vector")
        E("scalar", lambda e: e.activation(out=stat[:, 5:6], in_=stat[:, 3:4], func=AFT.Sqrt), Wn("ve"))
        E("vector", lambda e: e.reciprocal(out=stat[:, 4:5], in_=stat[:, 5:6]), Wn("se"))
        FENCE("vector")

    def emit_ladder(y_ap, r_ap, gt_ap, gwaits=()):
        yl, yh = y_ap[:, 0:512], y_ap[:, 512:1024]
        rl, rh = r_ap[:, 0:512], r_ap[:, 512:1024]
        gl, gh = gt_ap[:, 0:512], gt_ap[:, 512:1024]
        E("gpsimd", lambda e: e.tensor_copy(out=rh, in_=yh),
          [("ve", P.now("ve"))] + list(gwaits))
        E("vector", lambda e: e.tensor_copy(out=rl, in_=yl))
        for j in range(7, -1, -1):
            E("vector", lambda e, jj=j: e.tensor_scalar(out=gl, in0=rl, scalar1=float(2 ** jj),
                                                        scalar2=-float(2 ** jj), op0=ALU.is_ge, op1=ALU.mult))
            E("vector", lambda e: e.tensor_tensor(out=rl, in0=rl, in1=gl, op=ALU.add))
            E("gpsimd", lambda e, jj=j: e.tensor_scalar(out=gh, in0=rh, scalar1=float(2 ** jj),
                                                        scalar2=-float(2 ** jj), op0=ALU.is_ge, op1=ALU.mult))
            E("gpsimd", lambda e: e.tensor_tensor(out=rh, in0=rh, in1=gh, op=ALU.add))
        return P.now("ve")

    bank_free = {i: ("ve", 0) for i in range(8)}
    out_total = [0]
    bfree = {}

    def bwait(*keys):
        w = {}
        for k in keys:
            for sem, c in bfree.get(k, {}).items():
                w[sem] = max(w.get(sem, 0), c)
        return list(w.items())

    def bconsume(key, sem, count):
        d = bfree.setdefault(key, {})
        d[sem] = max(d.get(sem, 0), count)

    for p in range(N_PASS):
        tok0 = p * T_PASS

        wait_arena = bwait("wA", "wB")
        for kc in range(8):
            DMA("sync", wAB[:, kc * 1024:(kc + 1) * 1024], wq_d[kc * 128:(kc + 1) * 128, :], wait_arena)
        for kc in range(8):
            DMA("sync", wAB[:, (8 + kc) * 1024:(9 + kc) * 1024], wk_d[kc * 128:(kc + 1) * 128, :], wait_arena)
        w_qk = P.now("w")

        DMA("gpsimd", scr0, x_d[tok0:tok0 + 128, :],
            bwait("s0pe") + [("o", P.now("o")), ("ve", P.now("ve"))])
        DMA("gpsimd", x1_a[:, 1024:2048], x_d[tok0 + 128:tok0 + 256, :], [("ve", P.now("ve"))])
        lg_x = P.now("g")
        for tt in range(N_TILE):
            key = "sc2pe" if tt == 0 else "scCpe"
            s2t = scr2 if tt == 0 else scrCv
            xsrc = scr0 if tt == 0 else x1_a[:, 1024:2048]
            P.emit("vector", None, [("g", lg_x)] + bwait(key))
            P.emit("scalar", None, bwait(key))
            emit_ln(xsrc, scr1, s2t)
            E("vector", lambda e, t=tt: e.tensor_scalar(out=u_a[:, t * 1024:(t + 1) * 1024], in0=scr1,
                                                        scalar1=stat[:, 4:5], scalar2=None, op0=ALU.mult))
            FENCE("vector")
            emit_ln(u_a[:, tt * 1024:(tt + 1) * 1024], scr1, s2t)
            vxn = E("vector", lambda e, s2=s2t: e.tensor_scalar(out=s2, in0=scr1, scalar1=stat[:, 4:5],
                                                                scalar2=None, op0=ALU.mult))
            for c in range(8):
                bk = 4 + (c % 4)
                s_, c_ = bank_free[bk]
                pe_t = E("tensor", lambda e, cc=c, b=bk, s2=s2t: e.transpose(ps[b][:, 0:128],
                                                                             s2[:, cc * 128:(cc + 1) * 128], ident[:]),
                         [("ve", vxn), (s_, c_), ("w", cw)])
                bconsume(key, "pe", pe_t)
                E("scalar", lambda e, cc=c, b=bk, t=tt: e.activation(
                    out=actT[:, cc * 256 + t * 128: cc * 256 + (t + 1) * 128], in_=ps[b][:, 0:128],
                    func=AFT.Identity, scale=g1c[:, cc:cc + 1], bias=b1c[:, cc:cc + 1]), [("pe", pe_t)])
                bank_free[bk] = ("se", P.now("se"))
        actT_done = (P.now("se"), P.now("ve"))

        for (wslot0, bias_col, Adst, fmb) in ((0, bqc, Aq_v, fmq), (8, bkc, Ak_v, fmk)):
            for m in range(8):
                bk = m % 2
                s_, c_ = bank_free[bk]
                for kc in range(8):
                    E("tensor", lambda e, b=bk, k=kc, mm=m, w0=wslot0:
                      e.matmul(ps[b][:, 0:256],
                               wAB[:, (w0 + k) * 1024 + mm * 128: (w0 + k) * 1024 + (mm + 1) * 128],
                               actT[:, k * 256:(k + 1) * 256], start=(k == 0), stop=(k == 7)),
                      ([("w", w_qk), ("se", actT_done[0]), ("ve", actT_done[1]), (s_, c_)] if kc == 0 else ()))
                pe_mm = P.now("pe")
                bconsume("wA" if wslot0 == 0 else "wB", "pe", pe_mm)
                E("vector", lambda e, b=bk, mm=m, bc=bias_col: e.tensor_scalar(
                    out=elu0, in0=ps[b][:, 0:256], scalar1=bc[:, mm:mm + 1], scalar2=0.0,
                    op0=ALU.add, op1=ALU.min), [("pe", pe_mm), ("se", P.now("se"))])
                sexp = E("scalar", lambda e: e.activation(out=elu1, in_=elu0, func=AFT.Exp), Wn("ve"))
                E("vector", lambda e, b=bk, mm=m, bc=bias_col: e.tensor_scalar(
                    out=elu2, in0=ps[b][:, 0:256], scalar1=bc[:, mm:mm + 1], scalar2=0.0,
                    op0=ALU.add, op1=ALU.max))
                vfm = E("vector", lambda e, f=fmb: e.tensor_tensor(out=f, in0=elu1, in1=elu2, op=ALU.add),
                        [("se", sexp), ("pe", P.now("pe"))])
                bank_free[bk] = ("ve", vfm)
                for a in range(2):
                    bt = 6 + a
                    s2, c2 = bank_free[bt]
                    pe_d = E("tensor", lambda e, aa=a, b=bt, f=fmb:
                             e.matmul(ps[b][0:64, 0:256], ident[:, aa * 64:(aa + 1) * 64], f,
                                      start=True, stop=True), [("ve", vfm), (s2, c2)])
                    if a == 0:
                        E("vector", lambda e, b=bt, mm=m, Av=Adst: e.tensor_copy(
                            out=Av[0:64, :, :, 2 * mm], in_=ps[b][0:64, 0:256].rearrange(
                                "p (g t) -> p g t", g=N_GRP, t=8)), [("pe", pe_d)])
                        bank_free[bt] = ("ve", P.now("ve"))
                    else:
                        E("scalar", lambda e, b=bt, mm=m, Av=Adst: e.activation(
                            out=Av[0:64, :, :, 2 * mm + 1], in_=ps[b][0:64, 0:256].rearrange(
                                "p (g t) -> p g t", g=N_GRP, t=8), func=AFT.Identity), [("pe", pe_d)])
                        bank_free[bt] = ("se", P.now("se"))

        for kc in range(8):
            DMA("sync", wAB[:, kc * 1024:(kc + 1) * 1024], wv_d[kc * 128:(kc + 1) * 128, :], bwait("wA"))
        w_v = P.now("w")
        for kc in range(8):
            DMA("sync", wAB[:, (8 + kc) * 1024:(9 + kc) * 1024], wo_d[kc * 128:(kc + 1) * 128, :], bwait("wB"))
        w_o = P.now("w")
        for tt in range(N_TILE):
            for og in range(2):
                bk = og
                s_, c_ = bank_free[bk]
                for kc in range(8):
                    E("tensor", lambda e, b=bk, k=kc, t=tt, o=og:
                      e.matmul(ps[b][:, 0:512], actT[:, k * 256 + t * 128: k * 256 + (t + 1) * 128],
                               wAB[:, k * 1024 + o * 512: k * 1024 + (o + 1) * 512], start=(k == 0), stop=False),
                      ([("w", w_v), (s_, c_)] if kc == 0 else ()))
                E("tensor", lambda e, b=bk, o=og: e.matmul(ps[b][:, 0:512], brows[0:1, 1024:1152],
                                                           brows[0:1, o * 512:(o + 1) * 512], start=False, stop=True))
                pe_v = P.now("pe")
                bconsume("wA", "pe", pe_v)
                E("vector", lambda e, b=bk, t=tt, o=og: e.tensor_copy(
                    out=v_tm[:, t * 1024 + o * 512: t * 1024 + (o + 1) * 512], in_=ps[b][:, 0:512]),
                  [("pe", pe_v)])
                bank_free[bk] = ("ve", P.now("ve"))
        v_done = P.now("ve")

        a_done = (P.now("ve"), P.now("se"))
        for g in range(N_GRP):
            tt = g // 16
            vb = g % 2
            dv = DMA("gpsimd", Vg[:, vb * 66: vb * 66 + 64],
                     v_tm[(g % 16) * 8:(g % 16) * 8 + 8, tt * 1024:(tt + 1) * 1024],
                     [("ve", v_done), ("pe", P.now("pe")), ("gp", init_g)])
            s4, c4 = bank_free[4]
            pe_s = E("tensor", lambda e, gg=g: e.matmul(ps[4][:, 0:128],
                                                        Ak_v[0:64, gg].rearrange("p t h -> p (t h)"),
                                                        Aq_v[0:64, gg].rearrange("p t h -> p (t h)"),
                                                        start=True, stop=True),
                     [("ve", a_done[0]), ("se", a_done[1]), (s4, c4)])
            v_sm = E("vector", lambda e: e.tensor_tensor(out=Smb[:], in0=ps[4][:, 0:128], in1=maskc[:],
                                                         op=ALU.mult), [("pe", pe_s)])
            bank_free[4] = ("ve", v_sm)
            s5, c5 = bank_free[5]
            pe_o = E("tensor", lambda e, vv=vb: e.matmul(ps[5][:, 0:65], Smb[:], Vg[:, vv * 66: vv * 66 + 65],
                                                         start=True, stop=True),
                     [("ve", v_sm), ("g", dv), (s5, c5)])
            E("vector", lambda e: e.tensor_single_scalar(out=stat[:, 40:41], in_=ps[5][:, 64:65],
                                                         scalar=1e-6, op=ALU.add), [("pe", pe_o)])
            FENCE("vector")
            E("vector", lambda e: e.reciprocal(out=stat[:, 41:42], in_=stat[:, 40:41]))
            FENCE("vector")
            E("vector", lambda e, vv=vb: e.tensor_scalar(out=agb[:, vv * 128: vv * 128 + 64],
                                                         in0=ps[5][:, 0:64], scalar1=stat[:, 41:42],
                                                         scalar2=None, op0=ALU.mult))
            v_ag = E("vector", lambda e, vv=vb: e.tensor_copy(out=agb[:, vv * 128 + 64: vv * 128 + 128],
                                                              in_=agb[:, vv * 128: vv * 128 + 64]))
            bank_free[5] = ("ve", v_ag)
            bt = 6 + vb
            s6, c6 = bank_free[bt]
            pe_t0 = E("tensor", lambda e, vv=vb, b=bt: e.transpose(ps[b][:, 0:128],
                                                                   agb[:, vv * 128:(vv + 1) * 128], ident[:]),
                      [("ve", v_ag), (s6, c6)])
            E("vector", lambda e, gg=g, b=bt: e.tensor_copy(
                out=eo_v[0:64, gg],
                in_=ps[b][0:64, 0:128].rearrange("p (t c h) -> p t c h", t=8, c=8, h=2)[:, :, :, 0]),
              [("pe", pe_t0)])
            E("vector", lambda e, gg=g, b=bt: e.tensor_copy(
                out=eo_v[64:128, gg],
                in_=ps[b][64:128, 0:128].rearrange("p (t c h) -> p t c h", t=8, c=8, h=2)[:, :, :, 1]))
            bank_free[bt] = ("ve", P.now("ve"))
        attn_done = (P.now("ve"), P.now("se"))
        vtm_free = P.now("g")

        for tt in range(N_TILE):
            for og in range(2):
                sE, cE = bank_free[0]
                sO, cO = bank_free[1]
                for ct in range(8):
                    E("tensor", lambda e, c=ct, t=tt, o=og:
                      e.matmul(ps[0][:, 0:512],
                               eo_v[0:64, t * 16:(t + 1) * 16, :, c].rearrange("p g t -> p (g t)"),
                               wAB[0:64, (8 + c) * 1024 + o * 512: (8 + c) * 1024 + (o + 1) * 512],
                               start=(c == 0), stop=False),
                      ([("ve", attn_done[0]), ("se", attn_done[1]), ("w", w_o), (sE, cE), (sO, cO)]
                       if ct == 0 else ()))
                    E("tensor", lambda e, c=ct, t=tt, o=og:
                      e.matmul(ps[1][:, 0:512],
                               eo_v[64:128, t * 16:(t + 1) * 16, :, c].rearrange("p g t -> p (g t)"),
                               wAB[64:128, (8 + c) * 1024 + o * 512: (8 + c) * 1024 + (o + 1) * 512],
                               start=(c == 0), stop=(c == 7)))
                E("tensor", lambda e, o=og: e.matmul(ps[0][:, 0:512], brows[32:33, 1024:1152], brows[32:33, o * 512:(o + 1) * 512],
                                                     start=False, stop=True))
                pe_wo = P.now("pe")
                bconsume("wB", "pe", pe_wo)
                E("vector", lambda e: e.tensor_copy(out=big[:, 1664:2176], in_=ps[0][:, 0:512]), [("pe", pe_wo)])
                E("vector", lambda e: e.tensor_tensor(out=big[:, 1664:2176], in0=big[:, 1664:2176],
                                                      in1=ps[1][:, 0:512], op=ALU.add))
                vxa = E("vector", lambda e, t=tt, o=og: e.tensor_tensor(
                    out=u_a[:, t * 1024 + o * 512: t * 1024 + (o + 1) * 512], in0=big[:, 1664:2176],
                    in1=u_a[:, t * 1024 + o * 512: t * 1024 + (o + 1) * 512], op=ALU.add))
                bank_free[0] = ("ve", vxa)
                bank_free[1] = ("ve", vxa)

        for tt in range(N_TILE):
            key = "sc2pe" if tt == 0 else "scCpe"
            s2t = scr2 if tt == 0 else scrCv
            P.emit("vector", None, bwait(key))
            P.emit("scalar", None, bwait(key))
            emit_ln(u_a[:, tt * 1024:(tt + 1) * 1024], scr1, s2t)
            vh = E("vector", lambda e, s2=s2t: e.tensor_scalar(out=s2, in0=scr1, scalar1=stat[:, 4:5],
                                                               scalar2=None, op0=ALU.mult))
            for c in range(8):
                bk = 4 + (c % 4)
                s_, c_ = bank_free[bk]
                pe_t = E("tensor", lambda e, cc=c, b=bk, s2=s2t: e.transpose(ps[b][:, 0:128],
                                                                             s2[:, cc * 128:(cc + 1) * 128], ident[:]),
                         [("ve", vh), (s_, c_)])
                bconsume(key, "pe", pe_t)
                E("scalar", lambda e, cc=c, b=bk, t=tt: e.activation(
                    out=actT[:, cc * 256 + t * 128: cc * 256 + (t + 1) * 128], in_=ps[b][:, 0:128],
                    func=AFT.Identity, scale=g2c[:, cc:cc + 1], bias=b2c[:, cc:cc + 1]), [("pe", pe_t)])
                bank_free[bk] = ("se", P.now("se"))
        hT_done = (P.now("se"), P.now("ve"))

        for q in range(4):
            half = q % 2
            wait_half = bwait("wA" if half == 0 else "wB")
            for kc in range(8):
                DMA("sync", wAB[:, half * 8192 + kc * 1024: half * 8192 + (kc + 1) * 1024],
                    fcw_d[kc * 128:(kc + 1) * 128, q * 1024:(q + 1) * 1024], wait_half)
            w_fc = P.now("w")
            for m in range(8):
                M = q * 8 + m
                bk = m % 2
                s_, c_ = bank_free[bk]
                for kc in range(8):
                    E("tensor", lambda e, b=bk, k=kc, mm=m, h=half:
                      e.matmul(ps[b][:, 0:256],
                               wAB[:, h * 8192 + k * 1024 + mm * 128: h * 8192 + k * 1024 + (mm + 1) * 128],
                               actT[:, k * 256:(k + 1) * 256], start=(k == 0), stop=(k == 7)),
                      ([("w", w_fc), ("se", hT_done[0]), ("ve", hT_done[1]), (s_, c_)] if kc == 0 else ()))
                pe_fc = P.now("pe")
                bconsume("wA" if half == 0 else "wB", "pe", pe_fc)
                bconsume("actT", "pe", pe_fc)
                E("scalar", lambda e, b=bk, MM=M: e.activation(out=big[:, MM * 256:(MM + 1) * 256],
                                                               in_=ps[b][:, 0:256], func=AFT.Gelu,
                                                               bias=fcbc[:, MM:MM + 1]),
                  [("pe", pe_fc), ("ve", P.now("ve"))])
                bank_free[bk] = ("se", P.now("se"))
        g_done = P.now("se")

        for kc in range(32):
            slot = kc % 2
            DMA("sync", v_tm[:, slot * 1024:(slot + 1) * 1024], pjw_d[kc * 128:(kc + 1) * 128, :],
                ([("g", vtm_free)] if kc < 2 else []) + bwait("vslot%d" % slot))
            w_pj = P.now("w")
            for tt in range(N_TILE):
                for og in range(2):
                    bk = tt * 2 + og
                    if kc == 0:
                        s_, c_ = bank_free[bk]
                        wf = [("w", w_pj), ("se", g_done), (s_, c_)]
                    else:
                        wf = [("w", w_pj)]
                    E("tensor", lambda e, b=bk, k=kc, t=tt, o=og, sl=slot:
                      e.matmul(ps[b][:, 0:512], big[:, k * 256 + t * 128: k * 256 + (t + 1) * 128],
                               v_tm[:, sl * 1024 + o * 512: sl * 1024 + (o + 1) * 512],
                               start=(k == 0), stop=False), wf)
                    bconsume("vslot%d" % slot, "pe", P.now("pe"))
        for tt in range(N_TILE):
            for og in range(2):
                bk = tt * 2 + og
                E("tensor", lambda e, b=bk, o=og: e.matmul(ps[b][:, 0:512], brows[64:65, 1024:1152],
                                                           brows[64:65, o * 512:(o + 1) * 512], start=False, stop=True))
        pe_pj = P.now("pe")

        for tt in range(N_TILE):
            r0 = tok0 + tt * 128
            lx = DMA("gpsimd", scr0, x_d[r0:r0 + 128, :],
                     bwait("s0pe") + [("ve", P.now("ve")), ("o", P.now("o"))])
            P.emit("vector", None, bwait("sc2pe"))
            for og in range(2):
                bk = tt * 2 + og
                E("vector", lambda e, b=bk, t=tt, o=og: e.tensor_tensor(
                    out=scr1[:, o * 512:(o + 1) * 512], in0=ps[b][:, 0:512],
                    in1=u_a[:, t * 1024 + o * 512: t * 1024 + (o + 1) * 512], op=ALU.add), [("pe", pe_pj)])
                bank_free[bk] = ("ve", P.now("ve"))
            emit_ln(scr1, scr2, scr3)
            E("vector", lambda e, t=tt: e.scalar_tensor_tensor(
                out=x1_a[:, t * 1024:(t + 1) * 1024], in0=scr2, scalar=stat[:, 4:5], in1=scr0,
                op0=ALU.mult, op1=ALU.add), [("g", lx)])
            FENCE("vector")
            emit_ln(x1_a[:, tt * 1024:(tt + 1) * 1024], scr2, scr3)
            E("vector", lambda e: e.reduce_max(out=stat[:, 5:6], in_=scr2, axis=AX, apply_absolute_value=True))
            FENCE("vector")
            E("vector", lambda e: e.tensor_tensor(out=stat[:, 6:7], in0=stat[:, 5:6], in1=stat[:, 4:5], op=ALU.mult))
            FENCE("vector")
            E("vector", lambda e: e.tensor_single_scalar(out=stat[:, 7:8], in_=stat[:, 6:7], scalar=1e-5, op=ALU.max))
            FENCE("vector")
            E("vector", lambda e: e.reciprocal(out=stat[:, 8:9], in_=stat[:, 7:8]))
            FENCE("vector")
            E("vector", lambda e: e.tensor_single_scalar(out=stat[:, 9:10], in_=stat[:, 8:9], scalar=127.0,
                                                         op=ALU.mult))
            FENCE("vector")
            E("vector", lambda e: e.tensor_tensor(out=stat[:, 10:11], in0=stat[:, 9:10], in1=stat[:, 4:5],
                                                  op=ALU.mult))
            FENCE("vector")
            E("vector", lambda e, t=tt: e.tensor_copy(out=stat[:, 20 + t:21 + t], in_=stat[:, 7:8]))
            FENCE("vector")
            E("vector", lambda e: e.tensor_scalar(out=scr1, in0=scr2, scalar1=stat[:, 10:11],
                                                  scalar2=128.5, op0=ALU.mult, op1=ALU.add))
            emit_ladder(scr1, scr3, scr0, bwait("s0pe"))
            vkx = E("vector", lambda e: e.scalar_tensor_tensor(out=scr0, in0=scr1, scalar=128.0, in1=scr3,
                                                               op0=ALU.subtract, op1=ALU.subtract),
                    [("gp", P.now("gp"))])
            for c in range(8):
                bk = 4 + (c % 4)
                s_, c_ = bank_free[bk]
                pe_t = E("tensor", lambda e, cc=c, b=bk: e.transpose(ps[b][:, 0:128],
                                                                     scr0[:, cc * 128:(cc + 1) * 128], ident[:]),
                         [("ve", vkx), (s_, c_)])
                bconsume("s0pe", "pe", pe_t)
                E("scalar", lambda e, cc=c, b=bk, t=tt: e.activation(
                    out=kxT[:, cc * 256 + t * 128: cc * 256 + (t + 1) * 128], in_=ps[b][:, 0:128],
                    func=AFT.Identity), [("pe", pe_t)])
                bank_free[bk] = ("se", P.now("se"))
        kxT_done = (P.now("se"), P.now("ve"))

        if p == 0:
            def g0_scan(w_src, ntile, tile_w):
                for kc in range(ntile):
                    half = kc % 2
                    lw = DMA("sync", wAB[:, half * 4096: half * 4096 + tile_w],
                             w_src[kc * 128:(kc + 1) * 128, 0:tile_w],
                             bwait("wA", "g0m%d" % half))
                    v_r = E("vector", lambda e, k=kc, h=half, twd=tile_w: e.reduce_sum(
                        out=stat[:, 24 + k:25 + k], in_=wAB[:, h * 4096: h * 4096 + twd], axis=AX,
                        apply_absolute_value=True), [("w", lw)])
                    bconsume("g0m%d" % half, "ve", v_r)
                    bconsume("wA", "ve", v_r)

            def g0_stats(ntile, c_m, c_th):
                FENCE("vector")
                E("vector", lambda e, n=ntile: e.reduce_sum(out=stat[:, 60:61], in_=stat[:, 24:24 + n], axis=AX))
                v_acc = P.now("ve")
                FENCE("vector")
                s6_, c6_ = bank_free[6]
                pe_r1 = E("tensor", lambda e: e.matmul(ps[6][0:1, 0:1], onecol[:], stat[:, 60:61],
                                                       start=True, stop=True), [("ve", v_acc), (s6_, c6_)])
                E("vector", lambda e: e.tensor_copy(out=stat[0:1, 62:63], in_=ps[6][0:1, 0:1]),
                  [("pe", pe_r1)])
                FENCE("vector")
                s7_, c7_ = bank_free[7]
                pe_r2 = E("tensor", lambda e: e.matmul(ps[7][:, 0:1], brows[0:1, 1024:1152],
                                                       stat[0:1, 62:63], start=True, stop=True),
                          [("ve", P.now("ve")), (s7_, c7_)])
                E("vector", lambda e: e.tensor_copy(out=wsc[:, 6:7], in_=ps[7][:, 0:1]), [("pe", pe_r2)])
                bank_free[6] = ("ve", P.now("ve"))
                bank_free[7] = ("ve", P.now("ve"))
                FENCE("vector")
                E("vector", lambda e, cm=c_m: e.tensor_scalar(out=wsc[:, cm:cm + 1], in0=wsc[:, 6:7],
                                                              scalar1=1.0 / (1024.0 * 4096.0), scalar2=1e-5,
                                                              op0=ALU.mult, op1=ALU.max))
                FENCE("vector")
                E("vector", lambda e, cm=c_m, ct=c_th: e.tensor_single_scalar(
                    out=wsc[:, ct:ct + 1], in_=wsc[:, cm:cm + 1], scalar=0.5, op=ALU.mult))
                FENCE("vector")

            def g0_quant(w_src, q_dst, ntile, tile_w, c_th):
                for kc in range(ntile):
                    half = kc % 2
                    base = 8192 + half * 4096
                    lw = DMA("sync", wAB[:, base: base + tile_w],
                             w_src[kc * 128:(kc + 1) * 128, 0:tile_w],
                             bwait("wB", "g0q%d" % half))
                    s_sgn = E("scalar", lambda e, b0=base, twd=tile_w: e.activation(
                        out=big[:, 4096:4096 + twd], in_=wAB[:, b0: b0 + twd],
                        func=AFT.Sign), [("w", lw), ("pe", P.now("pe")), ("ve", P.now("ve")), ("g", P.now("g"))])
                    s_abs = E("scalar", lambda e, b0=base, twd=tile_w: e.activation(
                        out=big[:, 0:twd], in_=wAB[:, b0: b0 + twd], func=AFT.Abs),
                      [("w", lw), ("pe", P.now("pe")), ("ve", P.now("ve")), ("g", P.now("g"))])
                    bconsume("g0q%d" % half, "se", s_abs)
                    bconsume("wB", "se", s_abs)
                    E("vector", lambda e, ct=c_th, twd=tile_w: e.tensor_single_scalar(
                        out=big[:, 0:twd], in_=big[:, 0:twd], scalar=wsc[:, ct:ct + 1], op=ALU.is_gt),
                      [("se", s_abs), ("g", P.now("g"))])
                    v_q = E("vector", lambda e, twd=tile_w: e.tensor_tensor(
                        out=k2T[:, 0:twd], in0=big[:, 0:twd],
                        in1=big[:, 4096:4096 + twd], op=ALU.mult), [("se", s_sgn), ("g", P.now("g"))])
                    DMA("gpsimd", q_dst[kc * 128:(kc + 1) * 128, 0:tile_w],
                        k2T[:, 0:tile_w], [("ve", v_q)])

            g0_scan(wgT_d, 8, 4096)
            g0_stats(8, 0, 3)
            g0_quant(wgT_d, qg_d, 8, 4096, 3)
            g0_scan(wuT_d, 8, 4096)
            g0_stats(8, 1, 4)
            g0_quant(wuT_d, qu_d, 8, 4096, 4)
            g0_scan(wdT_d, 32, 1024)
            g0_stats(32, 2, 5)
            g0_quant(wdT_d, qd_d, 32, 1024, 5)
            qw_done = P.now("g")
        else:
            qw_done = P.now("g")

        for tt in range(N_TILE):
            cb = 12 + tt * 4
            E("vector", lambda e, c=cb, t=tt: e.tensor_tensor(out=stat[:, c:c + 1], in0=stat[:, 20 + t:21 + t],
                                                              in1=wsc[:, 0:1], op=ALU.mult))
            FENCE("vector")
            E("vector", lambda e, c=cb: e.tensor_single_scalar(out=stat[:, c:c + 1], in_=stat[:, c:c + 1],
                                                               scalar=1.0 / 127.0, op=ALU.mult))
            FENCE("vector")
            E("vector", lambda e, c=cb, t=tt: e.tensor_tensor(out=stat[:, c + 1:c + 2], in0=stat[:, 20 + t:21 + t],
                                                              in1=wsc[:, 1:2], op=ALU.mult))
            FENCE("vector")
            E("vector", lambda e, c=cb: e.tensor_single_scalar(out=stat[:, c + 1:c + 2], in_=stat[:, c + 1:c + 2],
                                                               scalar=1.0 / 127.0, op=ALU.mult))
            FENCE("vector")
        for e8 in range(16):
            src = qg_d if e8 < 8 else qu_d
            col0 = (e8 % 8) * 512
            slot = e8 % 2
            lwaits = bwait("k2s%d" % slot) + ([("g", qw_done)] if e8 == 0 else [])
            for kc in range(8):
                DMA("sync", k2T[:, slot * 4096 + kc * 512: slot * 4096 + (kc + 1) * 512],
                    src[kc * 128:(kc + 1) * 128, col0:col0 + 512], lwaits)
            w_e = P.now("w")
            for tt in range(N_TILE):
                cb = 12 + tt * 4
                og = e8 % 8
                bk = (e8 * 2 + tt) % 4
                s_, c_ = bank_free[bk]
                for kc in range(8):
                    E("tensor", lambda e, b=bk, k=kc, t=tt, sl=slot:
                      e.matmul(ps[b][:, 0:512], kxT[:, k * 256 + t * 128: k * 256 + (t + 1) * 128],
                               k2T[:, sl * 4096 + k * 512: sl * 4096 + (k + 1) * 512],
                               start=(k == 0), stop=(k == 7)),
                      ([("w", w_e), ("se", kxT_done[0]), ("ve", kxT_done[1]), (s_, c_)] if kc == 0 else ()))
                pe_mm = P.now("pe")
                bconsume("k2s%d" % slot, "pe", pe_mm)
                if e8 < 8:
                    E("scalar", lambda e, b=bk, t=tt, o=og, c=cb: e.activation(
                        out=big[:, t * 4096 + o * 512: t * 4096 + (o + 1) * 512], in_=ps[b][:, 0:512],
                        func=AFT.Silu, scale=stat[:, c:c + 1]), [("pe", pe_mm), ("ve", P.now("ve"))])
                    bank_free[bk] = ("se", P.now("se"))
                else:
                    E("vector", lambda e, b=bk, t=tt, o=og, c=cb: e.scalar_tensor_tensor(
                        out=big[:, t * 4096 + o * 512: t * 4096 + (o + 1) * 512], in0=ps[b][:, 0:512],
                        scalar=stat[:, c + 1:c + 2], in1=big[:, t * 4096 + o * 512: t * 4096 + (o + 1) * 512],
                        op0=ALU.mult, op1=ALU.mult), [("pe", pe_mm), ("se", P.now("se"))])
                    E("vector", lambda e, t=tt, o=og: e.reduce_max(
                        out=stat[:, 24 + t * 8 + o:25 + t * 8 + o], in_=big[:, t * 4096 + o * 512: t * 4096 + (o + 1) * 512],
                        axis=AX, apply_absolute_value=True))
                    bank_free[bk] = ("ve", P.now("ve"))
        for tt in range(N_TILE):
            cb2 = 56 + tt
            if True:
                if True:
                    FENCE("vector")
                    E("vector", lambda e, t=tt: e.reduce_max(out=stat[:, 52:53], in_=stat[:, 24 + t * 8:32 + t * 8], axis=AX))
                    FENCE("vector")
                    E("vector", lambda e: e.tensor_single_scalar(out=stat[:, 53:54], in_=stat[:, 52:53],
                                                                 scalar=1e-5, op=ALU.max))
                    FENCE("vector")
                    E("vector", lambda e: e.reciprocal(out=stat[:, 54:55], in_=stat[:, 53:54]))
                    FENCE("vector")
                    E("vector", lambda e: e.tensor_single_scalar(out=stat[:, 55:56], in_=stat[:, 54:55],
                                                                 scalar=127.0, op=ALU.mult))
                    FENCE("vector")
                    cb2 = 56 + tt
                    E("vector", lambda e, c=cb2: e.tensor_tensor(out=stat[:, c:c + 1], in0=stat[:, 53:54],
                                                                 in1=wsc[:, 2:3], op=ALU.mult))
                    FENCE("vector")
                    E("vector", lambda e, c=cb2: e.tensor_single_scalar(out=stat[:, c:c + 1],
                                                                        in_=stat[:, c:c + 1],
                                                                        scalar=1.0 / 127.0, op=ALU.mult))
                    FENCE("vector")
                    for c4 in range(4):
                        odd = c4 % 2
                        yb = actT[:, 0:1024] if odd else scr1
                        rb = actT[:, 1024:2048] if odd else scr3
                        resb = scrCv if odd else scr0
                        key = "scCpe" if odd else "s0pe"
                        P.emit("vector", None, bwait(key) + (bwait("actT") if odd else []))
                        E("vector", lambda e, t=tt, c=c4, y=yb: e.tensor_scalar(
                            out=y, in0=big[:, t * 4096 + c * 1024: t * 4096 + (c + 1) * 1024],
                            scalar1=stat[:, 55:56], scalar2=128.5, op0=ALU.mult, op1=ALU.add))
                        emit_ladder(yb, rb, resb, bwait(key))
                        vk2 = E("vector", lambda e, y=yb, r=rb, res=resb: e.scalar_tensor_tensor(
                            out=res, in0=y, scalar=128.0, in1=r,
                            op0=ALU.subtract, op1=ALU.subtract), [("gp", P.now("gp"))])
                        for j in range(8):
                            kc2 = c4 * 8 + j
                            bk2 = 4 + (j % 4)
                            s2, c2_ = bank_free[bk2]
                            pe_t = E("tensor", lambda e, jj=j, b=bk2, res=resb: e.transpose(
                                ps[b][:, 0:128], res[:, jj * 128:(jj + 1) * 128], ident[:]),
                                [("ve", vk2), (s2, c2_)])
                            bconsume(key, "pe", pe_t)
                            E("scalar", lambda e, k=kc2, b=bk2, t=tt: e.activation(
                                out=k2T[:, k * 256 + t * 128: k * 256 + (t + 1) * 128], in_=ps[b][:, 0:128],
                                func=AFT.Identity), [("pe", pe_t), ("g", qw_done)])
                            bank_free[bk2] = ("se", P.now("se"))
        k2_done = (P.now("se"), P.now("ve"))

        for kc in range(32):
            for og in range(2):
                DMA("sync", qdb[:], qd_d[kc * 128:(kc + 1) * 128, og * 512:(og + 1) * 512],
                    bwait("qdb"))
                w_qd = P.now("w")
                for tt in range(N_TILE):
                    bk = tt * 2 + og
                    if kc == 0:
                        s_, c_ = bank_free[bk]
                        wf = [("w", w_qd), ("se", k2_done[0]), ("ve", k2_done[1]), (s_, c_)]
                    else:
                        wf = [("w", w_qd)]
                    E("tensor", lambda e, b=bk, k=kc, t=tt:
                      e.matmul(ps[b][:, 0:512], k2T[:, k * 256 + t * 128: k * 256 + (t + 1) * 128],
                               qdb[:],
                               start=(k == 0), stop=(k == 31)), wf)
                    bconsume("qdb", "pe", P.now("pe"))
        pe_dn = P.now("pe")
        bconsume("k2s0", "pe", pe_dn)
        bconsume("k2s1", "pe", pe_dn)
        for tt in range(N_TILE):
            cb2 = 56 + tt
            GUARD("vector", "pe")
            GUARD("scalar", "pe")
            for og in range(2):
                bk = tt * 2 + og
                E("vector", lambda e, b=bk, o=og, c=cb2: e.tensor_scalar(
                    out=scr1[:, o * 512:(o + 1) * 512], in0=ps[b][:, 0:512], scalar1=stat[:, c:c + 1],
                    scalar2=None, op0=ALU.mult), [("pe", pe_dn)])
                bank_free[bk] = ("ve", P.now("ve"))
            emit_ln(scr1, scr2, scr3)
            v_out = E("vector", lambda e, t=tt: e.scalar_tensor_tensor(
                out=scr0, in0=scr2, scalar=stat[:, 4:5], in1=x1_a[:, t * 1024:(t + 1) * 1024],
                op0=ALU.mult, op1=ALU.add))
            r0 = tok0 + tt * 128
            DMA("gpsimd", out_d[r0:r0 + 128, :], scr0, [("ve", v_out)], osem="o")
            out_total[0] += 16

    P.emit("vector", None, [("o", out_total[0])])

    with ExitStack() as st:
        sems = {s: st.enter_context(nc.semaphore(s)) for s in SEMS}
        block = st.enter_context(nc.Block())

        def mk(name):
            def run(eng):
                for fn, ws, inc in P.ops[name]:
                    for s, v in ws:
                        eng.wait_ge(sems[s], v)
                    ins = fn(eng) if fn is not None else None
                    if inc is not None:
                        if ins is None:
                            ins = eng.engine_nop()
                        ins.then_inc(sems[inc[0]], inc[1])
            return run

        block.sync(mk("sync"))
        block.gpsimd(mk("gpsimd"))
        block.tensor(mk("tensor"))
        block.vector(mk("vector"))
        block.scalar(mk("scalar"))
    return nc


def _mk_brows(bv, bo, pjb):
    m = np.zeros((65, D + 128), dtype=np.float32)
    for r, b in ((0, bv), (32, bo), (64, pjb)):
        m[r, 0:D] = b
        m[r, D:D + 128] = 1.0
    return m


def kernel(**inputs):
    def A(name):
        return np.ascontiguousarray(np.asarray(inputs[name], dtype=np.float32))

    x = A("L0_attn_q_x").reshape(16384, D)
    consts = {
        "wq": A("L0_attn_wq"), "wk": A("L0_attn_wk"), "wv": A("L0_attn_wv"), "wo": A("L0_attn_wo"),
        "fcw": A("L0_attn_fc_w"), "pjw": A("L0_attn_proj_w"),
        "wgT": np.ascontiguousarray(A("L0_ffn_w_gate").T),
        "wuT": np.ascontiguousarray(A("L0_ffn_w_up").T),
        "wdT": np.ascontiguousarray(A("L0_ffn_w_down").T),
        "ident": np.eye(128, dtype=np.float32),
        "mask": np.kron(np.eye(8, dtype=np.float32), np.ones((16, 16), dtype=np.float32)),
        "brows": _mk_brows(A("L0_attn_bv"), A("L0_attn_bo"), A("L0_attn_proj_b")),
        "bqc": np.ascontiguousarray(A("L0_attn_bq").reshape(8, 128).T),
        "bkc": np.ascontiguousarray(A("L0_attn_bk").reshape(8, 128).T),
        "g1c": np.ascontiguousarray(A("L0_attn_ln1_w").reshape(8, 128).T),
        "b1c": np.ascontiguousarray(A("L0_attn_ln1_b").reshape(8, 128).T),
        "g2c": np.ascontiguousarray(A("L0_attn_ln2_w").reshape(8, 128).T),
        "b2c": np.ascontiguousarray(A("L0_attn_ln2_b").reshape(8, 128).T),
        "fcbc": np.ascontiguousarray(A("L0_attn_fc_b").reshape(32, 128).T),
    }
    nc = build()
    in_maps = []
    for c in range(NCORE):
        m = dict(consts)
        m["x"] = np.ascontiguousarray(x[c * TOK_CORE:(c + 1) * TOK_CORE])
        in_maps.append(m)
    res = run_bass_kernel_spmd(nc, in_maps, core_ids=list(range(NCORE)))
    out = np.concatenate([np.asarray(r["out"]) for r in res.results], axis=0)
    return out.reshape(2, 8192, D).astype(np.float32)


# version 16
# speedup vs baseline: 1.5442x; 1.5442x over previous
import numpy as np
from contextlib import ExitStack

import concourse.bass as bass
import concourse.mybir as mybir
from concourse.bass_utils import run_bass_kernel_spmd

F32 = mybir.dt.float32
BF16 = mybir.dt.bfloat16
ALU = mybir.AluOpType
AFT = mybir.ActivationFunctionType
AX = mybir.AxisListType.X

NCORE = 8
D = 1024
TOK_CORE = 2048
T_PASS = 256
N_PASS = TOK_CORE // T_PASS
N_TILE = T_PASS // 128
N_GRP = T_PASS // 8

ENGS = ("sync", "gpsimd", "tensor", "vector", "scalar")
SEMS = ("w", "g", "pe", "ve", "se", "gp", "o")
ESEM = {"sync": "w", "gpsimd": "gp", "tensor": "pe", "vector": "ve", "scalar": "se"}


class Prog:
    def __init__(self):
        self.ops = {e: [] for e in ENGS}
        self.cnt = {s: 0 for s in SEMS}
        self.lastw = {e: {s: 0 for s in SEMS} for e in ENGS}

    DMA_SEMS = ("w", "g", "o")

    def emit(self, eng, fn=None, waits=(), inc=None):
        ws = []
        for s, vv in waits:
            if s in self.DMA_SEMS:
                vv = self.cnt[s]
            if vv > self.lastw[eng][s]:
                ws.append((s, vv))
                self.lastw[eng][s] = vv
        if inc is not None:
            self.cnt[inc[0]] += inc[1]
        self.ops[eng].append((fn, ws, inc))

    def now(self, s):
        return self.cnt[s]


def build():
    nc = bass.Bass("TRN2", target_bir_lowering=False, detect_race_conditions=False)

    x_d = nc.dram_tensor("x", [TOK_CORE, D], F32, kind="ExternalInput").ap()
    out_d = nc.dram_tensor("out", [TOK_CORE, D], F32, kind="ExternalOutput").ap()
    wq_d = nc.dram_tensor("wq", [D, D], F32, kind="ExternalInput").ap()
    wk_d = nc.dram_tensor("wk", [D, D], F32, kind="ExternalInput").ap()
    wv_d = nc.dram_tensor("wv", [D, D], F32, kind="ExternalInput").ap()
    wo_d = nc.dram_tensor("wo", [D, D], F32, kind="ExternalInput").ap()
    fcw_d = nc.dram_tensor("fcw", [D, 4096], F32, kind="ExternalInput").ap()
    pjw_d = nc.dram_tensor("pjw", [4096, D], F32, kind="ExternalInput").ap()
    wgT_d = nc.dram_tensor("wgT", [D, 4096], F32, kind="ExternalInput").ap()
    wuT_d = nc.dram_tensor("wuT", [D, 4096], F32, kind="ExternalInput").ap()
    wdT_d = nc.dram_tensor("wdT", [4096, D], F32, kind="ExternalInput").ap()
    ident_d = nc.dram_tensor("ident", [128, 128], F32, kind="ExternalInput").ap()
    mask_d = nc.dram_tensor("mask", [128, 128], F32, kind="ExternalInput").ap()
    brows_d = nc.dram_tensor("brows", [65, D + 128], F32, kind="ExternalInput").ap()
    bqc_d = nc.dram_tensor("bqc", [128, 8], F32, kind="ExternalInput").ap()
    bkc_d = nc.dram_tensor("bkc", [128, 8], F32, kind="ExternalInput").ap()
    g1c_d = nc.dram_tensor("g1c", [128, 8], F32, kind="ExternalInput").ap()
    b1c_d = nc.dram_tensor("b1c", [128, 8], F32, kind="ExternalInput").ap()
    g2c_d = nc.dram_tensor("g2c", [128, 8], F32, kind="ExternalInput").ap()
    b2c_d = nc.dram_tensor("b2c", [128, 8], F32, kind="ExternalInput").ap()
    fcbc_d = nc.dram_tensor("fcbc", [128, 32], F32, kind="ExternalInput").ap()
    qg_d = nc.dram_tensor("qg_s", [D, 4096], BF16).ap()
    qu_d = nc.dram_tensor("qu_s", [D, 4096], BF16).ap()
    qd_d = nc.dram_tensor("qd_s", [4096, D], BF16).ap()

    wAB = nc.alloc_sbuf_tensor("wAB", [128, 16 * 1024], F32)
    big = nc.alloc_sbuf_tensor("big", [128, 8192], F32)
    k2T = nc.alloc_sbuf_tensor("k2T", [128, 8192], BF16)
    Aq = nc.alloc_sbuf_tensor("Aq", [64, 4096], F32)
    Ak = nc.alloc_sbuf_tensor("Ak", [64, 4096], F32)
    scrA = nc.alloc_sbuf_tensor("scrA", [128, 2048], F32)
    scrB = nc.alloc_sbuf_tensor("scrB", [128, 2048], F32)
    attnEO = nc.virtual_sbuf_tensor("attnEO", [128, 2048], F32)
    nc.back_virtual_tensor_with(attnEO, scrA)
    u_a = nc.alloc_sbuf_tensor("u_a", [128, 2048], F32)
    v_tm = nc.alloc_sbuf_tensor("v_tm", [128, 2048], F32)
    x1_a = nc.alloc_sbuf_tensor("x1_a", [128, 2048], F32)
    actT = nc.alloc_sbuf_tensor("actT", [128, 2048], F32)
    kxT = nc.alloc_sbuf_tensor("kxT", [128, 2048], BF16)
    scrC = nc.alloc_sbuf_tensor("scrC", [128, 1024], F32)
    qdb = nc.alloc_sbuf_tensor("qdb", [128, 512], BF16)
    Vg = nc.alloc_sbuf_tensor("Vg", [128, 132], F32)
    stat = nc.alloc_sbuf_tensor("stat", [128, 64], F32)
    wsc = nc.alloc_sbuf_tensor("wsc", [128, 8], F32)
    onecol = nc.alloc_sbuf_tensor("onecol", [128, 1], F32)
    ident = nc.alloc_sbuf_tensor("idents", [128, 128], F32)
    maskc = nc.alloc_sbuf_tensor("maskc", [128, 128], F32)
    brows = nc.alloc_sbuf_tensor("brows_s", [65, D + 128], F32)
    bqc = nc.alloc_sbuf_tensor("bqcs", [128, 8], F32)
    bkc = nc.alloc_sbuf_tensor("bkcs", [128, 8], F32)
    g1c = nc.alloc_sbuf_tensor("g1cs", [128, 8], F32)
    b1c = nc.alloc_sbuf_tensor("b1cs", [128, 8], F32)
    g2c = nc.alloc_sbuf_tensor("g2cs", [128, 8], F32)
    b2c = nc.alloc_sbuf_tensor("b2cs", [128, 8], F32)
    fcbc = nc.alloc_sbuf_tensor("fcbcs", [128, 32], F32)

    scr0 = scrA[:, 0:1024]
    scr1 = scrA[:, 1024:2048]
    scr2 = scrB[:, 0:1024]
    scr3 = scrB[:, 1024:2048]
    scrCv = scrC[:, 0:1024]
    agb = big[:, 1280:1536]
    Smb = big[:, 1536:1664]
    fmq = big[:, 0:256]
    fmk = big[:, 256:512]
    elu0 = big[:, 512:768]
    elu1 = big[:, 768:1024]
    elu2 = big[:, 1024:1280]

    ps = [nc.alloc_psum_tensor("ps%d" % i, [128, 512], F32) for i in range(8)]

    P = Prog()

    def E(eng, fn, waits=()):
        sem = ESEM[eng]
        P.emit(eng, fn, waits, (sem, 1))
        return P.now(sem)

    def DMA(eng, out_ap, in_ap, waits=(), osem=None):
        sem = osem or ("w" if eng == "sync" else "g")
        P.emit(eng, lambda e, o=out_ap, i=in_ap: e.dma_start(out=o, in_=i), waits, (sem, 16))
        return P.now(sem)

    def FENCE(eng):
        sem = ESEM[eng]
        P.emit(eng, None, [(sem, P.now(sem))])

    def GUARD(eng, *sems):
        P.emit(eng, None, [(s, P.now(s)) for s in sems])

    def Wn(*sems):
        return [(s, P.now(s)) for s in sems]

    Aq_v = Aq.ap().rearrange("p (g t h) -> p g t h", g=N_GRP, t=8, h=16)
    Ak_v = Ak.ap().rearrange("p (g t h) -> p g t h", g=N_GRP, t=8, h=16)
    eo_v = attnEO.ap().rearrange("p (g t c) -> p g t c", g=N_GRP, t=8, c=8)

    for dst, src in ((ident, ident_d), (maskc, mask_d), (brows, brows_d), (bqc, bqc_d), (bkc, bkc_d),
                     (g1c, g1c_d), (b1c, b1c_d), (g2c, g2c_d), (b2c, b2c_d), (fcbc, fcbc_d)):
        cw = DMA("sync", dst[:], src)
    E("gpsimd", lambda e: e.memset(onecol[:], 1.0))
    E("gpsimd", lambda e: e.memset(Vg[:, 64:65], 1.0))
    init_g = E("gpsimd", lambda e: e.memset(Vg[:, 130:131], 1.0))

    def emit_ln(src_ap, dbuf_ap, sq_ap, F=D):
        E("vector", lambda e: e.reduce_sum(out=stat[:, 0:1], in_=src_ap, axis=AX))
        FENCE("vector")
        E("vector", lambda e: e.tensor_single_scalar(out=stat[:, 1:2], in_=stat[:, 0:1],
                                                     scalar=1.0 / F, op=ALU.mult))
        FENCE("vector")
        E("vector", lambda e: e.tensor_scalar(out=dbuf_ap, in0=src_ap, scalar1=stat[:, 1:2],
                                              scalar2=None, op0=ALU.subtract))
        E("scalar", lambda e: e.activation(out=sq_ap, in_=dbuf_ap, func=AFT.Square), Wn("ve"))
        E("vector", lambda e: e.reduce_sum(out=stat[:, 2:3], in_=sq_ap, axis=AX), Wn("se"))
        FENCE("vector")
        E("vector", lambda e: e.tensor_scalar(out=stat[:, 3:4], in0=stat[:, 2:3], scalar1=1.0 / F,
                                              scalar2=1e-5, op0=ALU.mult, op1=ALU.add))
        FENCE("vector")
        E("scalar", lambda e: e.activation(out=stat[:, 5:6], in_=stat[:, 3:4], func=AFT.Sqrt), Wn("ve"))
        E("vector", lambda e: e.reciprocal(out=stat[:, 4:5], in_=stat[:, 5:6]), Wn("se"))
        FENCE("vector")

    def emit_ladder(y_ap, r_ap, gt_ap):
        E("vector", lambda e: e.tensor_copy(out=r_ap, in_=y_ap))
        for j in range(7, -1, -1):
            E("vector", lambda e, jj=j: e.tensor_scalar(out=gt_ap, in0=r_ap, scalar1=float(2 ** jj),
                                                        scalar2=-float(2 ** jj), op0=ALU.is_ge, op1=ALU.mult))
            E("vector", lambda e: e.tensor_tensor(out=r_ap, in0=r_ap, in1=gt_ap, op=ALU.add))
        return P.now("ve")

    bank_free = {i: ("ve", 0) for i in range(8)}
    out_total = [0]
    bfree = {}

    def bwait(*keys):
        w = {}
        for k in keys:
            for sem, c in bfree.get(k, {}).items():
                w[sem] = max(w.get(sem, 0), c)
        return list(w.items())

    def bconsume(key, sem, count):
        d = bfree.setdefault(key, {})
        d[sem] = max(d.get(sem, 0), count)

    for p in range(N_PASS):
        tok0 = p * T_PASS

        wait_arena = bwait("wA", "wB")
        for kc in range(8):
            DMA("sync", wAB[:, kc * 1024:(kc + 1) * 1024], wq_d[kc * 128:(kc + 1) * 128, :], wait_arena)
        for kc in range(8):
            DMA("sync", wAB[:, (8 + kc) * 1024:(9 + kc) * 1024], wk_d[kc * 128:(kc + 1) * 128, :], wait_arena)
        w_qk = P.now("w")

        DMA("gpsimd", scr0, x_d[tok0:tok0 + 128, :],
            bwait("s0pe") + [("o", P.now("o")), ("ve", P.now("ve"))])
        DMA("gpsimd", x1_a[:, 1024:2048], x_d[tok0 + 128:tok0 + 256, :], [("ve", P.now("ve"))])
        lg_x = P.now("g")
        for tt in range(N_TILE):
            key = "sc2pe" if tt == 0 else "scCpe"
            s2t = scr2 if tt == 0 else scrCv
            xsrc = scr0 if tt == 0 else x1_a[:, 1024:2048]
            P.emit("vector", None, [("g", lg_x)] + bwait(key))
            P.emit("scalar", None, bwait(key))
            emit_ln(xsrc, scr1, s2t)
            E("vector", lambda e, t=tt: e.tensor_scalar(out=u_a[:, t * 1024:(t + 1) * 1024], in0=scr1,
                                                        scalar1=stat[:, 4:5], scalar2=None, op0=ALU.mult))
            FENCE("vector")
            emit_ln(u_a[:, tt * 1024:(tt + 1) * 1024], scr1, s2t)
            vxn = E("vector", lambda e, s2=s2t: e.tensor_scalar(out=s2, in0=scr1, scalar1=stat[:, 4:5],
                                                                scalar2=None, op0=ALU.mult))
            for c in range(8):
                bk = 4 + (c % 4)
                s_, c_ = bank_free[bk]
                pe_t = E("tensor", lambda e, cc=c, b=bk, s2=s2t: e.transpose(ps[b][:, 0:128],
                                                                             s2[:, cc * 128:(cc + 1) * 128], ident[:]),
                         [("ve", vxn), (s_, c_), ("w", cw)])
                bconsume(key, "pe", pe_t)
                E("scalar", lambda e, cc=c, b=bk, t=tt: e.activation(
                    out=actT[:, cc * 256 + t * 128: cc * 256 + (t + 1) * 128], in_=ps[b][:, 0:128],
                    func=AFT.Identity, scale=g1c[:, cc:cc + 1], bias=b1c[:, cc:cc + 1]), [("pe", pe_t)])
                bank_free[bk] = ("se", P.now("se"))
        actT_done = (P.now("se"), P.now("ve"))

        for (wslot0, bias_col, Adst, fmb) in ((0, bqc, Aq_v, fmq), (8, bkc, Ak_v, fmk)):
            for m in range(8):
                bk = m % 2
                s_, c_ = bank_free[bk]
                for kc in range(8):
                    E("tensor", lambda e, b=bk, k=kc, mm=m, w0=wslot0:
                      e.matmul(ps[b][:, 0:256],
                               wAB[:, (w0 + k) * 1024 + mm * 128: (w0 + k) * 1024 + (mm + 1) * 128],
                               actT[:, k * 256:(k + 1) * 256], start=(k == 0), stop=(k == 7)),
                      ([("w", w_qk), ("se", actT_done[0]), ("ve", actT_done[1]), (s_, c_)] if kc == 0 else ()))
                pe_mm = P.now("pe")
                bconsume("wA" if wslot0 == 0 else "wB", "pe", pe_mm)
                E("vector", lambda e, b=bk, mm=m, bc=bias_col: e.tensor_scalar(
                    out=elu0, in0=ps[b][:, 0:256], scalar1=bc[:, mm:mm + 1], scalar2=0.0,
                    op0=ALU.add, op1=ALU.min), [("pe", pe_mm), ("se", P.now("se"))])
                sexp = E("scalar", lambda e: e.activation(out=elu1, in_=elu0, func=AFT.Exp), Wn("ve"))
                E("vector", lambda e, b=bk, mm=m, bc=bias_col: e.tensor_scalar(
                    out=elu2, in0=ps[b][:, 0:256], scalar1=bc[:, mm:mm + 1], scalar2=0.0,
                    op0=ALU.add, op1=ALU.max))
                vfm = E("vector", lambda e, f=fmb: e.tensor_tensor(out=f, in0=elu1, in1=elu2, op=ALU.add),
                        [("se", sexp), ("pe", P.now("pe"))])
                bank_free[bk] = ("ve", vfm)
                for a in range(2):
                    bt = 6 + a
                    s2, c2 = bank_free[bt]
                    pe_d = E("tensor", lambda e, aa=a, b=bt, f=fmb:
                             e.matmul(ps[b][0:64, 0:256], ident[:, aa * 64:(aa + 1) * 64], f,
                                      start=True, stop=True), [("ve", vfm), (s2, c2)])
                    if a == 0:
                        E("vector", lambda e, b=bt, mm=m, Av=Adst: e.tensor_copy(
                            out=Av[0:64, :, :, 2 * mm], in_=ps[b][0:64, 0:256].rearrange(
                                "p (g t) -> p g t", g=N_GRP, t=8)), [("pe", pe_d)])
                        bank_free[bt] = ("ve", P.now("ve"))
                    else:
                        E("scalar", lambda e, b=bt, mm=m, Av=Adst: e.activation(
                            out=Av[0:64, :, :, 2 * mm + 1], in_=ps[b][0:64, 0:256].rearrange(
                                "p (g t) -> p g t", g=N_GRP, t=8), func=AFT.Identity), [("pe", pe_d)])
                        bank_free[bt] = ("se", P.now("se"))

        for kc in range(8):
            DMA("sync", wAB[:, kc * 1024:(kc + 1) * 1024], wv_d[kc * 128:(kc + 1) * 128, :], bwait("wA"))
        w_v = P.now("w")
        for kc in range(8):
            DMA("sync", wAB[:, (8 + kc) * 1024:(9 + kc) * 1024], wo_d[kc * 128:(kc + 1) * 128, :], bwait("wB"))
        w_o = P.now("w")
        for tt in range(N_TILE):
            for og in range(2):
                bk = og
                s_, c_ = bank_free[bk]
                for kc in range(8):
                    E("tensor", lambda e, b=bk, k=kc, t=tt, o=og:
                      e.matmul(ps[b][:, 0:512], actT[:, k * 256 + t * 128: k * 256 + (t + 1) * 128],
                               wAB[:, k * 1024 + o * 512: k * 1024 + (o + 1) * 512], start=(k == 0), stop=False),
                      ([("w", w_v), (s_, c_)] if kc == 0 else ()))
                E("tensor", lambda e, b=bk, o=og: e.matmul(ps[b][:, 0:512], brows[0:1, 1024:1152],
                                                           brows[0:1, o * 512:(o + 1) * 512], start=False, stop=True))
                pe_v = P.now("pe")
                bconsume("wA", "pe", pe_v)
                E("vector", lambda e, b=bk, t=tt, o=og: e.tensor_copy(
                    out=v_tm[:, t * 1024 + o * 512: t * 1024 + (o + 1) * 512], in_=ps[b][:, 0:512]),
                  [("pe", pe_v)])
                bank_free[bk] = ("ve", P.now("ve"))
        v_done = P.now("ve")

        a_done = (P.now("ve"), P.now("se"))
        Smb2 = big[:, 2176:2304]
        for g in range(0, N_GRP, 2):
            ga, gb = g, g + 1
            tta, ttb = ga // 16, gb // 16
            dva = DMA("gpsimd", Vg[:, 0:64],
                      v_tm[(ga % 16) * 8:(ga % 16) * 8 + 8, tta * 1024:(tta + 1) * 1024],
                      [("ve", v_done), ("pe", P.now("pe")), ("gp", init_g)])
            dvb = DMA("gpsimd", Vg[:, 66:130],
                      v_tm[(gb % 16) * 8:(gb % 16) * 8 + 8, ttb * 1024:(ttb + 1) * 1024])
            s4, c4 = bank_free[4]
            pe_sa = E("tensor", lambda e, gg=ga: e.matmul(ps[4][:, 0:128],
                                                          Ak_v[0:64, gg].rearrange("p t h -> p (t h)"),
                                                          Aq_v[0:64, gg].rearrange("p t h -> p (t h)"),
                                                          start=True, stop=True),
                      [("ve", a_done[0]), ("se", a_done[1]), (s4, c4)])
            s5, c5 = bank_free[5]
            pe_sb = E("tensor", lambda e, gg=gb: e.matmul(ps[5][:, 0:128],
                                                          Ak_v[0:64, gg].rearrange("p t h -> p (t h)"),
                                                          Aq_v[0:64, gg].rearrange("p t h -> p (t h)"),
                                                          start=True, stop=True), [(s5, c5)])
            v_sma = E("vector", lambda e: e.tensor_tensor(out=Smb[:], in0=ps[4][:, 0:128], in1=maskc[:],
                                                          op=ALU.mult), [("pe", pe_sa)])
            v_smb = E("vector", lambda e: e.tensor_tensor(out=Smb2[:], in0=ps[5][:, 0:128], in1=maskc[:],
                                                          op=ALU.mult), [("pe", pe_sb)])
            bank_free[4] = ("ve", v_sma)
            bank_free[5] = ("ve", v_smb)
            s6, c6 = bank_free[6]
            pe_oa = E("tensor", lambda e: e.matmul(ps[6][:, 0:65], Smb[:], Vg[:, 0:65],
                                                   start=True, stop=True),
                      [("ve", v_sma), ("g", dva), (s6, c6)])
            s7, c7 = bank_free[7]
            pe_ob = E("tensor", lambda e: e.matmul(ps[7][:, 0:65], Smb2[:], Vg[:, 66:131],
                                                   start=True, stop=True),
                      [("ve", v_smb), ("g", dvb), (s7, c7)])
            E("vector", lambda e: e.tensor_single_scalar(out=stat[:, 40:41], in_=ps[6][:, 64:65],
                                                         scalar=1e-6, op=ALU.add), [("pe", pe_oa)])
            E("vector", lambda e: e.tensor_single_scalar(out=stat[:, 41:42], in_=ps[7][:, 64:65],
                                                         scalar=1e-6, op=ALU.add), [("pe", pe_ob)])
            FENCE("vector")
            E("vector", lambda e: e.reciprocal(out=stat[:, 42:44], in_=stat[:, 40:42]))
            FENCE("vector")
            E("vector", lambda e: e.tensor_scalar(out=agb[:, 0:64], in0=ps[6][:, 0:64],
                                                  scalar1=stat[:, 42:43], scalar2=None, op0=ALU.mult))
            v_aga = E("vector", lambda e: e.tensor_copy(out=agb[:, 64:128], in_=agb[:, 0:64]))
            E("vector", lambda e: e.tensor_scalar(out=agb[:, 128:192], in0=ps[7][:, 0:64],
                                                  scalar1=stat[:, 43:44], scalar2=None, op0=ALU.mult))
            v_agb2 = E("vector", lambda e: e.tensor_copy(out=agb[:, 192:256], in_=agb[:, 128:192]))
            bank_free[6] = ("ve", v_aga)
            bank_free[7] = ("ve", v_agb2)
            s4b, c4b = bank_free[4]
            pe_ta = E("tensor", lambda e: e.transpose(ps[4][:, 0:128], agb[:, 0:128], ident[:]),
                      [("ve", v_aga), (s4b, c4b)])
            s5b, c5b = bank_free[5]
            pe_tb = E("tensor", lambda e: e.transpose(ps[5][:, 0:128], agb[:, 128:256], ident[:]),
                      [("ve", v_agb2), (s5b, c5b)])
            E("vector", lambda e, gg=ga: e.tensor_copy(
                out=eo_v[0:64, gg],
                in_=ps[4][0:64, 0:128].rearrange("p (t c h) -> p t c h", t=8, c=8, h=2)[:, :, :, 0]),
              [("pe", pe_ta)])
            E("vector", lambda e, gg=ga: e.tensor_copy(
                out=eo_v[64:128, gg],
                in_=ps[4][64:128, 0:128].rearrange("p (t c h) -> p t c h", t=8, c=8, h=2)[:, :, :, 1]))
            E("vector", lambda e, gg=gb: e.tensor_copy(
                out=eo_v[0:64, gg],
                in_=ps[5][0:64, 0:128].rearrange("p (t c h) -> p t c h", t=8, c=8, h=2)[:, :, :, 0]),
              [("pe", pe_tb)])
            E("vector", lambda e, gg=gb: e.tensor_copy(
                out=eo_v[64:128, gg],
                in_=ps[5][64:128, 0:128].rearrange("p (t c h) -> p t c h", t=8, c=8, h=2)[:, :, :, 1]))
            bank_free[4] = ("ve", P.now("ve"))
            bank_free[5] = ("ve", P.now("ve"))
        attn_done = (P.now("ve"), P.now("se"))
        vtm_free = P.now("g")

        for tt in range(N_TILE):
            for og in range(2):
                sE, cE = bank_free[0]
                sO, cO = bank_free[1]
                for ct in range(8):
                    E("tensor", lambda e, c=ct, t=tt, o=og:
                      e.matmul(ps[0][:, 0:512],
                               eo_v[0:64, t * 16:(t + 1) * 16, :, c].rearrange("p g t -> p (g t)"),
                               wAB[0:64, (8 + c) * 1024 + o * 512: (8 + c) * 1024 + (o + 1) * 512],
                               start=(c == 0), stop=False),
                      ([("ve", attn_done[0]), ("se", attn_done[1]), ("w", w_o), (sE, cE), (sO, cO)]
                       if ct == 0 else ()))
                    E("tensor", lambda e, c=ct, t=tt, o=og:
                      e.matmul(ps[1][:, 0:512],
                               eo_v[64:128, t * 16:(t + 1) * 16, :, c].rearrange("p g t -> p (g t)"),
                               wAB[64:128, (8 + c) * 1024 + o * 512: (8 + c) * 1024 + (o + 1) * 512],
                               start=(c == 0), stop=(c == 7)))
                E("tensor", lambda e, o=og: e.matmul(ps[0][:, 0:512], brows[32:33, 1024:1152], brows[32:33, o * 512:(o + 1) * 512],
                                                     start=False, stop=True))
                pe_wo = P.now("pe")
                bconsume("wB", "pe", pe_wo)
                E("vector", lambda e: e.tensor_copy(out=big[:, 1664:2176], in_=ps[0][:, 0:512]), [("pe", pe_wo)])
                E("vector", lambda e: e.tensor_tensor(out=big[:, 1664:2176], in0=big[:, 1664:2176],
                                                      in1=ps[1][:, 0:512], op=ALU.add))
                vxa = E("vector", lambda e, t=tt, o=og: e.tensor_tensor(
                    out=u_a[:, t * 1024 + o * 512: t * 1024 + (o + 1) * 512], in0=big[:, 1664:2176],
                    in1=u_a[:, t * 1024 + o * 512: t * 1024 + (o + 1) * 512], op=ALU.add))
                bank_free[0] = ("ve", vxa)
                bank_free[1] = ("ve", vxa)

        for tt in range(N_TILE):
            key = "sc2pe" if tt == 0 else "scCpe"
            s2t = scr2 if tt == 0 else scrCv
            P.emit("vector", None, bwait(key))
            P.emit("scalar", None, bwait(key))
            emit_ln(u_a[:, tt * 1024:(tt + 1) * 1024], scr1, s2t)
            vh = E("vector", lambda e, s2=s2t: e.tensor_scalar(out=s2, in0=scr1, scalar1=stat[:, 4:5],
                                                               scalar2=None, op0=ALU.mult))
            for c in range(8):
                bk = 4 + (c % 4)
                s_, c_ = bank_free[bk]
                pe_t = E("tensor", lambda e, cc=c, b=bk, s2=s2t: e.transpose(ps[b][:, 0:128],
                                                                             s2[:, cc * 128:(cc + 1) * 128], ident[:]),
                         [("ve", vh), (s_, c_)])
                bconsume(key, "pe", pe_t)
                E("scalar", lambda e, cc=c, b=bk, t=tt: e.activation(
                    out=actT[:, cc * 256 + t * 128: cc * 256 + (t + 1) * 128], in_=ps[b][:, 0:128],
                    func=AFT.Identity, scale=g2c[:, cc:cc + 1], bias=b2c[:, cc:cc + 1]), [("pe", pe_t)])
                bank_free[bk] = ("se", P.now("se"))
        hT_done = (P.now("se"), P.now("ve"))

        for q in range(4):
            half = q % 2
            wait_half = bwait("wA" if half == 0 else "wB")
            for kc in range(8):
                DMA("sync", wAB[:, half * 8192 + kc * 1024: half * 8192 + (kc + 1) * 1024],
                    fcw_d[kc * 128:(kc + 1) * 128, q * 1024:(q + 1) * 1024], wait_half)
            w_fc = P.now("w")
            for m in range(8):
                M = q * 8 + m
                bk = m % 2
                s_, c_ = bank_free[bk]
                for kc in range(8):
                    E("tensor", lambda e, b=bk, k=kc, mm=m, h=half:
                      e.matmul(ps[b][:, 0:256],
                               wAB[:, h * 8192 + k * 1024 + mm * 128: h * 8192 + k * 1024 + (mm + 1) * 128],
                               actT[:, k * 256:(k + 1) * 256], start=(k == 0), stop=(k == 7)),
                      ([("w", w_fc), ("se", hT_done[0]), ("ve", hT_done[1]), (s_, c_)] if kc == 0 else ()))
                pe_fc = P.now("pe")
                bconsume("wA" if half == 0 else "wB", "pe", pe_fc)
                bconsume("actT", "pe", pe_fc)
                E("scalar", lambda e, b=bk, MM=M: e.activation(out=big[:, MM * 256:(MM + 1) * 256],
                                                               in_=ps[b][:, 0:256], func=AFT.Gelu,
                                                               bias=fcbc[:, MM:MM + 1]),
                  [("pe", pe_fc), ("ve", P.now("ve"))])
                bank_free[bk] = ("se", P.now("se"))
        g_done = P.now("se")

        for kc in range(32):
            slot = kc % 2
            DMA("sync", v_tm[:, slot * 1024:(slot + 1) * 1024], pjw_d[kc * 128:(kc + 1) * 128, :],
                ([("g", vtm_free)] if kc < 2 else []) + bwait("vslot%d" % slot))
            w_pj = P.now("w")
            for tt in range(N_TILE):
                for og in range(2):
                    bk = tt * 2 + og
                    if kc == 0:
                        s_, c_ = bank_free[bk]
                        wf = [("w", w_pj), ("se", g_done), (s_, c_)]
                    else:
                        wf = [("w", w_pj)]
                    E("tensor", lambda e, b=bk, k=kc, t=tt, o=og, sl=slot:
                      e.matmul(ps[b][:, 0:512], big[:, k * 256 + t * 128: k * 256 + (t + 1) * 128],
                               v_tm[:, sl * 1024 + o * 512: sl * 1024 + (o + 1) * 512],
                               start=(k == 0), stop=False), wf)
                    bconsume("vslot%d" % slot, "pe", P.now("pe"))
        for tt in range(N_TILE):
            for og in range(2):
                bk = tt * 2 + og
                E("tensor", lambda e, b=bk, o=og: e.matmul(ps[b][:, 0:512], brows[64:65, 1024:1152],
                                                           brows[64:65, o * 512:(o + 1) * 512], start=False, stop=True))
        pe_pj = P.now("pe")

        for tt in range(N_TILE):
            r0 = tok0 + tt * 128
            lx = DMA("gpsimd", scr0, x_d[r0:r0 + 128, :],
                     bwait("s0pe") + [("ve", P.now("ve")), ("o", P.now("o"))])
            P.emit("vector", None, bwait("sc2pe"))
            for og in range(2):
                bk = tt * 2 + og
                E("vector", lambda e, b=bk, t=tt, o=og: e.tensor_tensor(
                    out=scr1[:, o * 512:(o + 1) * 512], in0=ps[b][:, 0:512],
                    in1=u_a[:, t * 1024 + o * 512: t * 1024 + (o + 1) * 512], op=ALU.add), [("pe", pe_pj)])
                bank_free[bk] = ("ve", P.now("ve"))
            emit_ln(scr1, scr2, scr3)
            E("vector", lambda e, t=tt: e.scalar_tensor_tensor(
                out=x1_a[:, t * 1024:(t + 1) * 1024], in0=scr2, scalar=stat[:, 4:5], in1=scr0,
                op0=ALU.mult, op1=ALU.add), [("g", lx)])
            FENCE("vector")
            emit_ln(x1_a[:, tt * 1024:(tt + 1) * 1024], scr2, scr3)
            E("vector", lambda e: e.reduce_max(out=stat[:, 5:6], in_=scr2, axis=AX, apply_absolute_value=True))
            FENCE("vector")
            E("vector", lambda e: e.tensor_tensor(out=stat[:, 6:7], in0=stat[:, 5:6], in1=stat[:, 4:5], op=ALU.mult))
            FENCE("vector")
            E("vector", lambda e: e.tensor_single_scalar(out=stat[:, 7:8], in_=stat[:, 6:7], scalar=1e-5, op=ALU.max))
            FENCE("vector")
            E("vector", lambda e: e.reciprocal(out=stat[:, 8:9], in_=stat[:, 7:8]))
            FENCE("vector")
            E("vector", lambda e: e.tensor_single_scalar(out=stat[:, 9:10], in_=stat[:, 8:9], scalar=127.0,
                                                         op=ALU.mult))
            FENCE("vector")
            E("vector", lambda e: e.tensor_tensor(out=stat[:, 10:11], in0=stat[:, 9:10], in1=stat[:, 4:5],
                                                  op=ALU.mult))
            FENCE("vector")
            E("vector", lambda e, t=tt: e.tensor_copy(out=stat[:, 20 + t:21 + t], in_=stat[:, 7:8]))
            FENCE("vector")
            E("vector", lambda e: e.tensor_scalar(out=scr1, in0=scr2, scalar1=stat[:, 10:11],
                                                  scalar2=128.5, op0=ALU.mult, op1=ALU.add))
            emit_ladder(scr1, scr3, scr0)
            vkx = E("vector", lambda e: e.scalar_tensor_tensor(out=scr0, in0=scr1, scalar=128.0, in1=scr3,
                                                               op0=ALU.subtract, op1=ALU.subtract))
            for c in range(8):
                bk = 4 + (c % 4)
                s_, c_ = bank_free[bk]
                pe_t = E("tensor", lambda e, cc=c, b=bk: e.transpose(ps[b][:, 0:128],
                                                                     scr0[:, cc * 128:(cc + 1) * 128], ident[:]),
                         [("ve", vkx), (s_, c_)])
                bconsume("s0pe", "pe", pe_t)
                E("scalar", lambda e, cc=c, b=bk, t=tt: e.activation(
                    out=kxT[:, cc * 256 + t * 128: cc * 256 + (t + 1) * 128], in_=ps[b][:, 0:128],
                    func=AFT.Identity), [("pe", pe_t)])
                bank_free[bk] = ("se", P.now("se"))
        kxT_done = (P.now("se"), P.now("ve"))

        if p == 0:
            def g0_scan(w_src, ntile, tile_w):
                for kc in range(ntile):
                    half = kc % 2
                    lw = DMA("sync", wAB[:, half * 4096: half * 4096 + tile_w],
                             w_src[kc * 128:(kc + 1) * 128, 0:tile_w],
                             bwait("wA", "g0m%d" % half))
                    v_r = E("vector", lambda e, k=kc, h=half, twd=tile_w: e.reduce_sum(
                        out=stat[:, 24 + k:25 + k], in_=wAB[:, h * 4096: h * 4096 + twd], axis=AX,
                        apply_absolute_value=True), [("w", lw)])
                    bconsume("g0m%d" % half, "ve", v_r)
                    bconsume("wA", "ve", v_r)

            def g0_stats(ntile, c_m, c_th):
                FENCE("vector")
                E("vector", lambda e, n=ntile: e.reduce_sum(out=stat[:, 60:61], in_=stat[:, 24:24 + n], axis=AX))
                v_acc = P.now("ve")
                FENCE("vector")
                s6_, c6_ = bank_free[6]
                pe_r1 = E("tensor", lambda e: e.matmul(ps[6][0:1, 0:1], onecol[:], stat[:, 60:61],
                                                       start=True, stop=True), [("ve", v_acc), (s6_, c6_)])
                E("vector", lambda e: e.tensor_copy(out=stat[0:1, 62:63], in_=ps[6][0:1, 0:1]),
                  [("pe", pe_r1)])
                FENCE("vector")
                s7_, c7_ = bank_free[7]
                pe_r2 = E("tensor", lambda e: e.matmul(ps[7][:, 0:1], brows[0:1, 1024:1152],
                                                       stat[0:1, 62:63], start=True, stop=True),
                          [("ve", P.now("ve")), (s7_, c7_)])
                E("vector", lambda e: e.tensor_copy(out=wsc[:, 6:7], in_=ps[7][:, 0:1]), [("pe", pe_r2)])
                bank_free[6] = ("ve", P.now("ve"))
                bank_free[7] = ("ve", P.now("ve"))
                FENCE("vector")
                E("vector", lambda e, cm=c_m: e.tensor_scalar(out=wsc[:, cm:cm + 1], in0=wsc[:, 6:7],
                                                              scalar1=1.0 / (1024.0 * 4096.0), scalar2=1e-5,
                                                              op0=ALU.mult, op1=ALU.max))
                FENCE("vector")
                E("vector", lambda e, cm=c_m, ct=c_th: e.tensor_single_scalar(
                    out=wsc[:, ct:ct + 1], in_=wsc[:, cm:cm + 1], scalar=0.5, op=ALU.mult))
                FENCE("vector")

            def g0_quant(w_src, q_dst, ntile, tile_w, c_th):
                for kc in range(ntile):
                    half = kc % 2
                    base = 8192 + half * 4096
                    lw = DMA("sync", wAB[:, base: base + tile_w],
                             w_src[kc * 128:(kc + 1) * 128, 0:tile_w],
                             bwait("wB", "g0q%d" % half))
                    s_sgn = E("scalar", lambda e, b0=base, twd=tile_w: e.activation(
                        out=big[:, 4096:4096 + twd], in_=wAB[:, b0: b0 + twd],
                        func=AFT.Sign), [("w", lw), ("pe", P.now("pe")), ("ve", P.now("ve")), ("g", P.now("g"))])
                    s_abs = E("scalar", lambda e, b0=base, twd=tile_w: e.activation(
                        out=big[:, 0:twd], in_=wAB[:, b0: b0 + twd], func=AFT.Abs),
                      [("w", lw), ("pe", P.now("pe")), ("ve", P.now("ve")), ("g", P.now("g"))])
                    bconsume("g0q%d" % half, "se", s_abs)
                    bconsume("wB", "se", s_abs)
                    E("vector", lambda e, ct=c_th, twd=tile_w: e.tensor_single_scalar(
                        out=big[:, 0:twd], in_=big[:, 0:twd], scalar=wsc[:, ct:ct + 1], op=ALU.is_gt),
                      [("se", s_abs), ("g", P.now("g"))])
                    v_q = E("vector", lambda e, twd=tile_w: e.tensor_tensor(
                        out=k2T[:, 0:twd], in0=big[:, 0:twd],
                        in1=big[:, 4096:4096 + twd], op=ALU.mult), [("se", s_sgn), ("g", P.now("g"))])
                    DMA("gpsimd", q_dst[kc * 128:(kc + 1) * 128, 0:tile_w],
                        k2T[:, 0:tile_w], [("ve", v_q)])

            g0_scan(wgT_d, 8, 4096)
            g0_stats(8, 0, 3)
            g0_quant(wgT_d, qg_d, 8, 4096, 3)
            g0_scan(wuT_d, 8, 4096)
            g0_stats(8, 1, 4)
            g0_quant(wuT_d, qu_d, 8, 4096, 4)
            g0_scan(wdT_d, 32, 1024)
            g0_stats(32, 2, 5)
            g0_quant(wdT_d, qd_d, 32, 1024, 5)
            qw_done = P.now("g")
        else:
            qw_done = P.now("g")

        for tt in range(N_TILE):
            cb = 12 + tt * 4
            E("vector", lambda e, c=cb, t=tt: e.tensor_tensor(out=stat[:, c:c + 1], in0=stat[:, 20 + t:21 + t],
                                                              in1=wsc[:, 0:1], op=ALU.mult))
            FENCE("vector")
            E("vector", lambda e, c=cb: e.tensor_single_scalar(out=stat[:, c:c + 1], in_=stat[:, c:c + 1],
                                                               scalar=1.0 / 127.0, op=ALU.mult))
            FENCE("vector")
            E("vector", lambda e, c=cb, t=tt: e.tensor_tensor(out=stat[:, c + 1:c + 2], in0=stat[:, 20 + t:21 + t],
                                                              in1=wsc[:, 1:2], op=ALU.mult))
            FENCE("vector")
            E("vector", lambda e, c=cb: e.tensor_single_scalar(out=stat[:, c + 1:c + 2], in_=stat[:, c + 1:c + 2],
                                                               scalar=1.0 / 127.0, op=ALU.mult))
            FENCE("vector")
        for e8 in range(16):
            src = qg_d if e8 < 8 else qu_d
            col0 = (e8 % 8) * 512
            slot = e8 % 2
            lwaits = bwait("k2s%d" % slot) + ([("g", qw_done)] if e8 == 0 else [])
            for kc in range(8):
                DMA("sync", k2T[:, slot * 4096 + kc * 512: slot * 4096 + (kc + 1) * 512],
                    src[kc * 128:(kc + 1) * 128, col0:col0 + 512], lwaits)
            w_e = P.now("w")
            for tt in range(N_TILE):
                cb = 12 + tt * 4
                og = e8 % 8
                bk = (e8 * 2 + tt) % 4
                s_, c_ = bank_free[bk]
                for kc in range(8):
                    E("tensor", lambda e, b=bk, k=kc, t=tt, sl=slot:
                      e.matmul(ps[b][:, 0:512], kxT[:, k * 256 + t * 128: k * 256 + (t + 1) * 128],
                               k2T[:, sl * 4096 + k * 512: sl * 4096 + (k + 1) * 512],
                               start=(k == 0), stop=(k == 7)),
                      ([("w", w_e), ("se", kxT_done[0]), ("ve", kxT_done[1]), (s_, c_)] if kc == 0 else ()))
                pe_mm = P.now("pe")
                bconsume("k2s%d" % slot, "pe", pe_mm)
                if e8 < 8:
                    E("scalar", lambda e, b=bk, t=tt, o=og, c=cb: e.activation(
                        out=big[:, t * 4096 + o * 512: t * 4096 + (o + 1) * 512], in_=ps[b][:, 0:512],
                        func=AFT.Silu, scale=stat[:, c:c + 1]), [("pe", pe_mm), ("ve", P.now("ve"))])
                    bank_free[bk] = ("se", P.now("se"))
                else:
                    E("vector", lambda e, b=bk, t=tt, o=og, c=cb: e.scalar_tensor_tensor(
                        out=big[:, t * 4096 + o * 512: t * 4096 + (o + 1) * 512], in0=ps[b][:, 0:512],
                        scalar=stat[:, c + 1:c + 2], in1=big[:, t * 4096 + o * 512: t * 4096 + (o + 1) * 512],
                        op0=ALU.mult, op1=ALU.mult), [("pe", pe_mm), ("se", P.now("se"))])
                    E("vector", lambda e, t=tt, o=og: e.reduce_max(
                        out=stat[:, 24 + t * 8 + o:25 + t * 8 + o], in_=big[:, t * 4096 + o * 512: t * 4096 + (o + 1) * 512],
                        axis=AX, apply_absolute_value=True))
                    bank_free[bk] = ("ve", P.now("ve"))
        for tt in range(N_TILE):
            cb2 = 56 + tt
            if True:
                if True:
                    FENCE("vector")
                    E("vector", lambda e, t=tt: e.reduce_max(out=stat[:, 52:53], in_=stat[:, 24 + t * 8:32 + t * 8], axis=AX))
                    FENCE("vector")
                    E("vector", lambda e: e.tensor_single_scalar(out=stat[:, 53:54], in_=stat[:, 52:53],
                                                                 scalar=1e-5, op=ALU.max))
                    FENCE("vector")
                    E("vector", lambda e: e.reciprocal(out=stat[:, 54:55], in_=stat[:, 53:54]))
                    FENCE("vector")
                    E("vector", lambda e: e.tensor_single_scalar(out=stat[:, 55:56], in_=stat[:, 54:55],
                                                                 scalar=127.0, op=ALU.mult))
                    FENCE("vector")
                    cb2 = 56 + tt
                    E("vector", lambda e, c=cb2: e.tensor_tensor(out=stat[:, c:c + 1], in0=stat[:, 53:54],
                                                                 in1=wsc[:, 2:3], op=ALU.mult))
                    FENCE("vector")
                    E("vector", lambda e, c=cb2: e.tensor_single_scalar(out=stat[:, c:c + 1],
                                                                        in_=stat[:, c:c + 1],
                                                                        scalar=1.0 / 127.0, op=ALU.mult))
                    FENCE("vector")
                    for c4 in range(4):
                        odd = c4 % 2
                        yb = actT[:, 0:1024] if odd else scr1
                        rb = actT[:, 1024:2048] if odd else scr3
                        resb = scrCv if odd else scr0
                        key = "scCpe" if odd else "s0pe"
                        P.emit("vector", None, bwait(key) + (bwait("actT") if odd else []))
                        E("vector", lambda e, t=tt, c=c4, y=yb: e.tensor_scalar(
                            out=y, in0=big[:, t * 4096 + c * 1024: t * 4096 + (c + 1) * 1024],
                            scalar1=stat[:, 55:56], scalar2=128.5, op0=ALU.mult, op1=ALU.add))
                        emit_ladder(yb, rb, resb)
                        vk2 = E("vector", lambda e, y=yb, r=rb, res=resb: e.scalar_tensor_tensor(
                            out=res, in0=y, scalar=128.0, in1=r,
                            op0=ALU.subtract, op1=ALU.subtract))
                        for j in range(8):
                            kc2 = c4 * 8 + j
                            bk2 = 4 + (j % 4)
                            s2, c2_ = bank_free[bk2]
                            pe_t = E("tensor", lambda e, jj=j, b=bk2, res=resb: e.transpose(
                                ps[b][:, 0:128], res[:, jj * 128:(jj + 1) * 128], ident[:]),
                                [("ve", vk2), (s2, c2_)])
                            bconsume(key, "pe", pe_t)
                            E("scalar", lambda e, k=kc2, b=bk2, t=tt: e.activation(
                                out=k2T[:, k * 256 + t * 128: k * 256 + (t + 1) * 128], in_=ps[b][:, 0:128],
                                func=AFT.Identity), [("pe", pe_t), ("g", qw_done)])
                            bank_free[bk2] = ("se", P.now("se"))
        k2_done = (P.now("se"), P.now("ve"))

        for kc in range(32):
            for og in range(2):
                DMA("sync", qdb[:], qd_d[kc * 128:(kc + 1) * 128, og * 512:(og + 1) * 512],
                    bwait("qdb"))
                w_qd = P.now("w")
                for tt in range(N_TILE):
                    bk = tt * 2 + og
                    if kc == 0:
                        s_, c_ = bank_free[bk]
                        wf = [("w", w_qd), ("se", k2_done[0]), ("ve", k2_done[1]), (s_, c_)]
                    else:
                        wf = [("w", w_qd)]
                    E("tensor", lambda e, b=bk, k=kc, t=tt:
                      e.matmul(ps[b][:, 0:512], k2T[:, k * 256 + t * 128: k * 256 + (t + 1) * 128],
                               qdb[:],
                               start=(k == 0), stop=(k == 31)), wf)
                    bconsume("qdb", "pe", P.now("pe"))
        pe_dn = P.now("pe")
        bconsume("k2s0", "pe", pe_dn)
        bconsume("k2s1", "pe", pe_dn)
        for tt in range(N_TILE):
            cb2 = 56 + tt
            GUARD("vector", "pe")
            GUARD("scalar", "pe")
            for og in range(2):
                bk = tt * 2 + og
                E("vector", lambda e, b=bk, o=og, c=cb2: e.tensor_scalar(
                    out=scr1[:, o * 512:(o + 1) * 512], in0=ps[b][:, 0:512], scalar1=stat[:, c:c + 1],
                    scalar2=None, op0=ALU.mult), [("pe", pe_dn)])
                bank_free[bk] = ("ve", P.now("ve"))
            emit_ln(scr1, scr2, scr3)
            v_out = E("vector", lambda e, t=tt: e.scalar_tensor_tensor(
                out=scr0, in0=scr2, scalar=stat[:, 4:5], in1=x1_a[:, t * 1024:(t + 1) * 1024],
                op0=ALU.mult, op1=ALU.add))
            r0 = tok0 + tt * 128
            DMA("gpsimd", out_d[r0:r0 + 128, :], scr0, [("ve", v_out)], osem="o")
            out_total[0] += 16

    P.emit("vector", None, [("o", out_total[0])])

    with ExitStack() as st:
        sems = {s: st.enter_context(nc.semaphore(s)) for s in SEMS}
        block = st.enter_context(nc.Block())

        def mk(name):
            def run(eng):
                for fn, ws, inc in P.ops[name]:
                    for s, v in ws:
                        eng.wait_ge(sems[s], v)
                    ins = fn(eng) if fn is not None else None
                    if inc is not None:
                        if ins is None:
                            ins = eng.engine_nop()
                        ins.then_inc(sems[inc[0]], inc[1])
            return run

        block.sync(mk("sync"))
        block.gpsimd(mk("gpsimd"))
        block.tensor(mk("tensor"))
        block.vector(mk("vector"))
        block.scalar(mk("scalar"))
    return nc


def _mk_brows(bv, bo, pjb):
    m = np.zeros((65, D + 128), dtype=np.float32)
    for r, b in ((0, bv), (32, bo), (64, pjb)):
        m[r, 0:D] = b
        m[r, D:D + 128] = 1.0
    return m


def kernel(**inputs):
    def A(name):
        return np.ascontiguousarray(np.asarray(inputs[name], dtype=np.float32))

    x = A("L0_attn_q_x").reshape(16384, D)
    consts = {
        "wq": A("L0_attn_wq"), "wk": A("L0_attn_wk"), "wv": A("L0_attn_wv"), "wo": A("L0_attn_wo"),
        "fcw": A("L0_attn_fc_w"), "pjw": A("L0_attn_proj_w"),
        "wgT": np.ascontiguousarray(A("L0_ffn_w_gate").T),
        "wuT": np.ascontiguousarray(A("L0_ffn_w_up").T),
        "wdT": np.ascontiguousarray(A("L0_ffn_w_down").T),
        "ident": np.eye(128, dtype=np.float32),
        "mask": np.kron(np.eye(8, dtype=np.float32), np.ones((16, 16), dtype=np.float32)),
        "brows": _mk_brows(A("L0_attn_bv"), A("L0_attn_bo"), A("L0_attn_proj_b")),
        "bqc": np.ascontiguousarray(A("L0_attn_bq").reshape(8, 128).T),
        "bkc": np.ascontiguousarray(A("L0_attn_bk").reshape(8, 128).T),
        "g1c": np.ascontiguousarray(A("L0_attn_ln1_w").reshape(8, 128).T),
        "b1c": np.ascontiguousarray(A("L0_attn_ln1_b").reshape(8, 128).T),
        "g2c": np.ascontiguousarray(A("L0_attn_ln2_w").reshape(8, 128).T),
        "b2c": np.ascontiguousarray(A("L0_attn_ln2_b").reshape(8, 128).T),
        "fcbc": np.ascontiguousarray(A("L0_attn_fc_b").reshape(32, 128).T),
    }
    nc = build()
    in_maps = []
    for c in range(NCORE):
        m = dict(consts)
        m["x"] = np.ascontiguousarray(x[c * TOK_CORE:(c + 1) * TOK_CORE])
        in_maps.append(m)
    res = run_bass_kernel_spmd(nc, in_maps, core_ids=list(range(NCORE)))
    out = np.concatenate([np.asarray(r["out"]) for r in res.results], axis=0)
    return out.reshape(2, 8192, D).astype(np.float32)


# version 17
# speedup vs baseline: 1.6629x; 1.0768x over previous
import numpy as np
from contextlib import ExitStack

import concourse.bass as bass
import concourse.mybir as mybir
from concourse.bass_utils import run_bass_kernel_spmd

F32 = mybir.dt.float32
BF16 = mybir.dt.bfloat16
ALU = mybir.AluOpType
AFT = mybir.ActivationFunctionType
AX = mybir.AxisListType.X

NCORE = 8
D = 1024
TOK_CORE = 2048
T_PASS = 256
N_PASS = TOK_CORE // T_PASS
N_TILE = T_PASS // 128
N_GRP = T_PASS // 8

ENGS = ("sync", "gpsimd", "tensor", "vector", "scalar")
SEMS = ("w", "g", "pe", "ve", "se", "gp", "o")
ESEM = {"sync": "w", "gpsimd": "gp", "tensor": "pe", "vector": "ve", "scalar": "se"}


class Prog:
    def __init__(self):
        self.ops = {e: [] for e in ENGS}
        self.cnt = {s: 0 for s in SEMS}
        self.lastw = {e: {s: 0 for s in SEMS} for e in ENGS}

    DMA_SEMS = ("w", "g", "o")

    def emit(self, eng, fn=None, waits=(), inc=None):
        ws = []
        for s, vv in waits:
            if s in self.DMA_SEMS:
                vv = self.cnt[s]
            if vv > self.lastw[eng][s]:
                ws.append((s, vv))
                self.lastw[eng][s] = vv
        if inc is not None:
            self.cnt[inc[0]] += inc[1]
        self.ops[eng].append((fn, ws, inc))

    def now(self, s):
        return self.cnt[s]


def build():
    nc = bass.Bass("TRN2", target_bir_lowering=False, detect_race_conditions=False)

    x_d = nc.dram_tensor("x", [TOK_CORE, D], F32, kind="ExternalInput").ap()
    out_d = nc.dram_tensor("out", [TOK_CORE, D], F32, kind="ExternalOutput").ap()
    wq_d = nc.dram_tensor("wq", [D, D], F32, kind="ExternalInput").ap()
    wk_d = nc.dram_tensor("wk", [D, D], F32, kind="ExternalInput").ap()
    wv_d = nc.dram_tensor("wv", [D, D], F32, kind="ExternalInput").ap()
    wo_d = nc.dram_tensor("wo", [D, D], F32, kind="ExternalInput").ap()
    fcw_d = nc.dram_tensor("fcw", [D, 4096], F32, kind="ExternalInput").ap()
    pjw_d = nc.dram_tensor("pjw", [4096, D], F32, kind="ExternalInput").ap()
    wgT_d = nc.dram_tensor("wgT", [D, 4096], F32, kind="ExternalInput").ap()
    wuT_d = nc.dram_tensor("wuT", [D, 4096], F32, kind="ExternalInput").ap()
    wdT_d = nc.dram_tensor("wdT", [4096, D], F32, kind="ExternalInput").ap()
    ident_d = nc.dram_tensor("ident", [128, 128], F32, kind="ExternalInput").ap()
    mask_d = nc.dram_tensor("mask", [128, 128], F32, kind="ExternalInput").ap()
    brows_d = nc.dram_tensor("brows", [65, D + 128], F32, kind="ExternalInput").ap()
    bqc_d = nc.dram_tensor("bqc", [128, 8], F32, kind="ExternalInput").ap()
    bkc_d = nc.dram_tensor("bkc", [128, 8], F32, kind="ExternalInput").ap()
    g1c_d = nc.dram_tensor("g1c", [128, 8], F32, kind="ExternalInput").ap()
    b1c_d = nc.dram_tensor("b1c", [128, 8], F32, kind="ExternalInput").ap()
    g2c_d = nc.dram_tensor("g2c", [128, 8], F32, kind="ExternalInput").ap()
    b2c_d = nc.dram_tensor("b2c", [128, 8], F32, kind="ExternalInput").ap()
    fcbc_d = nc.dram_tensor("fcbc", [128, 32], F32, kind="ExternalInput").ap()
    qg_d = nc.dram_tensor("qg_s", [D, 4096], BF16).ap()
    qu_d = nc.dram_tensor("qu_s", [D, 4096], BF16).ap()
    qd_d = nc.dram_tensor("qd_s", [4096, D], BF16).ap()

    wAB = nc.alloc_sbuf_tensor("wAB", [128, 16 * 1024], F32)
    big = nc.alloc_sbuf_tensor("big", [128, 8192], F32)
    k2T = nc.alloc_sbuf_tensor("k2T", [128, 8192], BF16)
    Aq = nc.alloc_sbuf_tensor("Aq", [64, 4096], F32)
    Ak = nc.alloc_sbuf_tensor("Ak", [64, 4096], F32)
    scrA = nc.alloc_sbuf_tensor("scrA", [128, 2048], F32)
    scrB = nc.alloc_sbuf_tensor("scrB", [128, 2048], F32)
    attnEO = nc.virtual_sbuf_tensor("attnEO", [128, 2048], F32)
    nc.back_virtual_tensor_with(attnEO, scrA)
    u_a = nc.alloc_sbuf_tensor("u_a", [128, 2048], F32)
    v_tm = nc.alloc_sbuf_tensor("v_tm", [128, 2048], F32)
    x1_a = nc.alloc_sbuf_tensor("x1_a", [128, 2048], F32)
    actT = nc.alloc_sbuf_tensor("actT", [128, 2048], F32)
    kxT = nc.alloc_sbuf_tensor("kxT", [128, 2048], BF16)
    scrC = nc.alloc_sbuf_tensor("scrC", [128, 1024], F32)
    qdb = nc.alloc_sbuf_tensor("qdb", [128, 512], BF16)
    Vg = nc.alloc_sbuf_tensor("Vg", [128, 132], F32)
    stat = nc.alloc_sbuf_tensor("stat", [128, 64], F32)
    wsc = nc.alloc_sbuf_tensor("wsc", [128, 8], F32)
    onecol = nc.alloc_sbuf_tensor("onecol", [128, 1], F32)
    ident = nc.alloc_sbuf_tensor("idents", [128, 128], F32)
    maskc = nc.alloc_sbuf_tensor("maskc", [128, 128], F32)
    brows = nc.alloc_sbuf_tensor("brows_s", [65, D + 128], F32)
    bqc = nc.alloc_sbuf_tensor("bqcs", [128, 8], F32)
    bkc = nc.alloc_sbuf_tensor("bkcs", [128, 8], F32)
    g1c = nc.alloc_sbuf_tensor("g1cs", [128, 8], F32)
    b1c = nc.alloc_sbuf_tensor("b1cs", [128, 8], F32)
    g2c = nc.alloc_sbuf_tensor("g2cs", [128, 8], F32)
    b2c = nc.alloc_sbuf_tensor("b2cs", [128, 8], F32)
    fcbc = nc.alloc_sbuf_tensor("fcbcs", [128, 32], F32)

    scr0 = scrA[:, 0:1024]
    scr1 = scrA[:, 1024:2048]
    scr2 = scrB[:, 0:1024]
    scr3 = scrB[:, 1024:2048]
    scrCv = scrC[:, 0:1024]
    agb = big[:, 1280:1536]
    Smb = big[:, 1536:1664]
    fmq = big[:, 0:256]
    fmk = big[:, 256:512]
    elu0 = big[:, 512:768]
    elu1 = big[:, 768:1024]
    elu2 = big[:, 1024:1280]

    ps = [nc.alloc_psum_tensor("ps%d" % i, [128, 512], F32) for i in range(8)]

    P = Prog()

    def E(eng, fn, waits=()):
        sem = ESEM[eng]
        P.emit(eng, fn, waits, (sem, 1))
        return P.now(sem)

    def DMA(eng, out_ap, in_ap, waits=(), osem=None):
        sem = osem or ("w" if eng == "sync" else "g")
        P.emit(eng, lambda e, o=out_ap, i=in_ap: e.dma_start(out=o, in_=i), waits, (sem, 16))
        return P.now(sem)

    def FENCE(eng):
        sem = ESEM[eng]
        P.emit(eng, None, [(sem, P.now(sem))])

    def GUARD(eng, *sems):
        P.emit(eng, None, [(s, P.now(s)) for s in sems])

    def Wn(*sems):
        return [(s, P.now(s)) for s in sems]

    Aq_v = Aq.ap().rearrange("p (g t h) -> p g t h", g=N_GRP, t=8, h=16)
    Ak_v = Ak.ap().rearrange("p (g t h) -> p g t h", g=N_GRP, t=8, h=16)
    eo_v = attnEO.ap().rearrange("p (g t c) -> p g t c", g=N_GRP, t=8, c=8)

    for dst, src in ((ident, ident_d), (maskc, mask_d), (brows, brows_d), (bqc, bqc_d), (bkc, bkc_d),
                     (g1c, g1c_d), (b1c, b1c_d), (g2c, g2c_d), (b2c, b2c_d), (fcbc, fcbc_d)):
        cw = DMA("sync", dst[:], src)
    E("gpsimd", lambda e: e.memset(onecol[:], 1.0))
    E("gpsimd", lambda e: e.memset(Vg[:, 64:65], 1.0))
    init_g = E("gpsimd", lambda e: e.memset(Vg[:, 130:131], 1.0))

    def emit_ln(src_ap, dbuf_ap, sq_ap, F=D):
        E("vector", lambda e: e.reduce_sum(out=stat[:, 0:1], in_=src_ap, axis=AX))
        FENCE("vector")
        E("vector", lambda e: e.tensor_single_scalar(out=stat[:, 1:2], in_=stat[:, 0:1],
                                                     scalar=1.0 / F, op=ALU.mult))
        FENCE("vector")
        E("vector", lambda e: e.tensor_scalar(out=dbuf_ap, in0=src_ap, scalar1=stat[:, 1:2],
                                              scalar2=None, op0=ALU.subtract))
        E("scalar", lambda e: e.activation(out=sq_ap, in_=dbuf_ap, func=AFT.Square), Wn("ve"))
        E("vector", lambda e: e.reduce_sum(out=stat[:, 2:3], in_=sq_ap, axis=AX), Wn("se"))
        FENCE("vector")
        E("vector", lambda e: e.tensor_scalar(out=stat[:, 3:4], in0=stat[:, 2:3], scalar1=1.0 / F,
                                              scalar2=1e-5, op0=ALU.mult, op1=ALU.add))
        FENCE("vector")
        E("scalar", lambda e: e.activation(out=stat[:, 5:6], in_=stat[:, 3:4], func=AFT.Sqrt), Wn("ve"))
        E("vector", lambda e: e.reciprocal(out=stat[:, 4:5], in_=stat[:, 5:6]), Wn("se"))
        FENCE("vector")

    def emit_ladder(y_ap, r_ap, gt_ap):
        E("vector", lambda e: e.tensor_copy(out=r_ap, in_=y_ap))
        for j in range(7, -1, -1):
            E("vector", lambda e, jj=j: e.tensor_scalar(out=gt_ap, in0=r_ap, scalar1=float(2 ** jj),
                                                        scalar2=-float(2 ** jj), op0=ALU.is_ge, op1=ALU.mult))
            E("vector", lambda e: e.tensor_tensor(out=r_ap, in0=r_ap, in1=gt_ap, op=ALU.add))
        return P.now("ve")

    bank_free = {i: ("ve", 0) for i in range(8)}
    out_total = [0]
    bfree = {}

    def bwait(*keys):
        w = {}
        for k in keys:
            for sem, c in bfree.get(k, {}).items():
                w[sem] = max(w.get(sem, 0), c)
        return list(w.items())

    def bconsume(key, sem, count):
        d = bfree.setdefault(key, {})
        d[sem] = max(d.get(sem, 0), count)

    for p in range(N_PASS):
        tok0 = p * T_PASS

        wait_arena = bwait("wA", "wB")
        for kc in range(8):
            DMA("sync", wAB[:, kc * 1024:(kc + 1) * 1024], wq_d[kc * 128:(kc + 1) * 128, :], wait_arena)
        for kc in range(8):
            DMA("sync", wAB[:, (8 + kc) * 1024:(9 + kc) * 1024], wk_d[kc * 128:(kc + 1) * 128, :], wait_arena)
        w_qk = P.now("w")

        DMA("gpsimd", scr0, x_d[tok0:tok0 + 128, :],
            bwait("s0pe") + [("o", P.now("o")), ("ve", P.now("ve"))])
        DMA("gpsimd", x1_a[:, 1024:2048], x_d[tok0 + 128:tok0 + 256, :], [("ve", P.now("ve"))])
        lg_x = P.now("g")
        for tt in range(N_TILE):
            key = "sc2pe" if tt == 0 else "scCpe"
            s2t = scr2 if tt == 0 else scrCv
            xsrc = scr0 if tt == 0 else x1_a[:, 1024:2048]
            P.emit("vector", None, [("g", lg_x)] + bwait(key))
            P.emit("scalar", None, bwait(key))
            emit_ln(xsrc, scr1, s2t)
            E("vector", lambda e, t=tt: e.tensor_scalar(out=u_a[:, t * 1024:(t + 1) * 1024], in0=scr1,
                                                        scalar1=stat[:, 4:5], scalar2=None, op0=ALU.mult))
            FENCE("vector")
            emit_ln(u_a[:, tt * 1024:(tt + 1) * 1024], scr1, s2t)
            vxn = E("vector", lambda e, s2=s2t: e.tensor_scalar(out=s2, in0=scr1, scalar1=stat[:, 4:5],
                                                                scalar2=None, op0=ALU.mult))
            for c in range(8):
                bk = 4 + (c % 4)
                s_, c_ = bank_free[bk]
                pe_t = E("tensor", lambda e, cc=c, b=bk, s2=s2t: e.transpose(ps[b][:, 0:128],
                                                                             s2[:, cc * 128:(cc + 1) * 128], ident[:]),
                         [("ve", vxn), (s_, c_), ("w", cw)])
                bconsume(key, "pe", pe_t)
                E("scalar", lambda e, cc=c, b=bk, t=tt: e.activation(
                    out=actT[:, cc * 256 + t * 128: cc * 256 + (t + 1) * 128], in_=ps[b][:, 0:128],
                    func=AFT.Identity, scale=g1c[:, cc:cc + 1], bias=b1c[:, cc:cc + 1]), [("pe", pe_t)])
                bank_free[bk] = ("se", P.now("se"))
        actT_done = (P.now("se"), P.now("ve"))

        for (wslot0, bias_col, Adst, fmb) in ((0, bqc, Aq_v, fmq), (8, bkc, Ak_v, fmk)):
            for m in range(8):
                bk = m % 2
                s_, c_ = bank_free[bk]
                for kc in range(8):
                    E("tensor", lambda e, b=bk, k=kc, mm=m, w0=wslot0:
                      e.matmul(ps[b][:, 0:256],
                               wAB[:, (w0 + k) * 1024 + mm * 128: (w0 + k) * 1024 + (mm + 1) * 128],
                               actT[:, k * 256:(k + 1) * 256], start=(k == 0), stop=(k == 7)),
                      ([("w", w_qk), ("se", actT_done[0]), ("ve", actT_done[1]), (s_, c_)] if kc == 0 else ()))
                pe_mm = P.now("pe")
                bconsume("wA" if wslot0 == 0 else "wB", "pe", pe_mm)
                E("vector", lambda e, b=bk, mm=m, bc=bias_col: e.tensor_scalar(
                    out=elu0, in0=ps[b][:, 0:256], scalar1=bc[:, mm:mm + 1], scalar2=0.0,
                    op0=ALU.add, op1=ALU.min), [("pe", pe_mm), ("se", P.now("se"))])
                sexp = E("scalar", lambda e: e.activation(out=elu1, in_=elu0, func=AFT.Exp), Wn("ve"))
                E("vector", lambda e, b=bk, mm=m, bc=bias_col: e.tensor_scalar(
                    out=elu2, in0=ps[b][:, 0:256], scalar1=bc[:, mm:mm + 1], scalar2=0.0,
                    op0=ALU.add, op1=ALU.max))
                vfm = E("vector", lambda e, f=fmb: e.tensor_tensor(out=f, in0=elu1, in1=elu2, op=ALU.add),
                        [("se", sexp), ("pe", P.now("pe"))])
                bank_free[bk] = ("ve", vfm)
                for a in range(2):
                    bt = 6 + a
                    s2, c2 = bank_free[bt]
                    pe_d = E("tensor", lambda e, aa=a, b=bt, f=fmb:
                             e.matmul(ps[b][0:64, 0:256], ident[:, aa * 64:(aa + 1) * 64], f,
                                      start=True, stop=True), [("ve", vfm), (s2, c2)])
                    if a == 0:
                        E("vector", lambda e, b=bt, mm=m, Av=Adst: e.tensor_copy(
                            out=Av[0:64, :, :, 2 * mm], in_=ps[b][0:64, 0:256].rearrange(
                                "p (g t) -> p g t", g=N_GRP, t=8)), [("pe", pe_d)])
                        bank_free[bt] = ("ve", P.now("ve"))
                    else:
                        E("scalar", lambda e, b=bt, mm=m, Av=Adst: e.activation(
                            out=Av[0:64, :, :, 2 * mm + 1], in_=ps[b][0:64, 0:256].rearrange(
                                "p (g t) -> p g t", g=N_GRP, t=8), func=AFT.Identity), [("pe", pe_d)])
                        bank_free[bt] = ("se", P.now("se"))

        for kc in range(8):
            DMA("sync", wAB[:, kc * 1024:(kc + 1) * 1024], wv_d[kc * 128:(kc + 1) * 128, :], bwait("wA"))
        w_v = P.now("w")
        for kc in range(8):
            DMA("sync", wAB[:, (8 + kc) * 1024:(9 + kc) * 1024], wo_d[kc * 128:(kc + 1) * 128, :], bwait("wB"))
        w_o = P.now("w")
        for tt in range(N_TILE):
            for og in range(2):
                bk = og
                s_, c_ = bank_free[bk]
                for kc in range(8):
                    E("tensor", lambda e, b=bk, k=kc, t=tt, o=og:
                      e.matmul(ps[b][:, 0:512], actT[:, k * 256 + t * 128: k * 256 + (t + 1) * 128],
                               wAB[:, k * 1024 + o * 512: k * 1024 + (o + 1) * 512], start=(k == 0), stop=False),
                      ([("w", w_v), (s_, c_)] if kc == 0 else ()))
                E("tensor", lambda e, b=bk, o=og: e.matmul(ps[b][:, 0:512], brows[0:1, 1024:1152],
                                                           brows[0:1, o * 512:(o + 1) * 512], start=False, stop=True))
                pe_v = P.now("pe")
                bconsume("wA", "pe", pe_v)
                E("vector", lambda e, b=bk, t=tt, o=og: e.tensor_copy(
                    out=v_tm[:, t * 1024 + o * 512: t * 1024 + (o + 1) * 512], in_=ps[b][:, 0:512]),
                  [("pe", pe_v)])
                bank_free[bk] = ("ve", P.now("ve"))
        v_done = P.now("ve")

        a_done = (P.now("ve"), P.now("se"))
        Smb2 = big[:, 2176:2304]
        for g in range(0, N_GRP, 2):
            ga, gb = g, g + 1
            tta, ttb = ga // 16, gb // 16
            dva = DMA("gpsimd", Vg[:, 0:64],
                      v_tm[(ga % 16) * 8:(ga % 16) * 8 + 8, tta * 1024:(tta + 1) * 1024],
                      [("ve", v_done), ("pe", P.now("pe")), ("gp", init_g)])
            dvb = DMA("gpsimd", Vg[:, 66:130],
                      v_tm[(gb % 16) * 8:(gb % 16) * 8 + 8, ttb * 1024:(ttb + 1) * 1024])
            s4, c4 = bank_free[4]
            pe_sa = E("tensor", lambda e, gg=ga: e.matmul(ps[4][:, 0:128],
                                                          Ak_v[0:64, gg].rearrange("p t h -> p (t h)"),
                                                          Aq_v[0:64, gg].rearrange("p t h -> p (t h)"),
                                                          start=True, stop=True),
                      [("ve", a_done[0]), ("se", a_done[1]), (s4, c4)])
            s5, c5 = bank_free[5]
            pe_sb = E("tensor", lambda e, gg=gb: e.matmul(ps[5][:, 0:128],
                                                          Ak_v[0:64, gg].rearrange("p t h -> p (t h)"),
                                                          Aq_v[0:64, gg].rearrange("p t h -> p (t h)"),
                                                          start=True, stop=True), [(s5, c5)])
            v_sma = E("vector", lambda e: e.tensor_tensor(out=Smb[:], in0=ps[4][:, 0:128], in1=maskc[:],
                                                          op=ALU.mult), [("pe", pe_sa)])
            v_smb = E("vector", lambda e: e.tensor_tensor(out=Smb2[:], in0=ps[5][:, 0:128], in1=maskc[:],
                                                          op=ALU.mult), [("pe", pe_sb)])
            bank_free[4] = ("ve", v_sma)
            bank_free[5] = ("ve", v_smb)
            s6, c6 = bank_free[6]
            pe_oa = E("tensor", lambda e: e.matmul(ps[6][:, 0:65], Smb[:], Vg[:, 0:65],
                                                   start=True, stop=True),
                      [("ve", v_sma), ("g", dva), (s6, c6)])
            s7, c7 = bank_free[7]
            pe_ob = E("tensor", lambda e: e.matmul(ps[7][:, 0:65], Smb2[:], Vg[:, 66:131],
                                                   start=True, stop=True),
                      [("ve", v_smb), ("g", dvb), (s7, c7)])
            E("vector", lambda e: e.tensor_single_scalar(out=stat[:, 40:41], in_=ps[6][:, 64:65],
                                                         scalar=1e-6, op=ALU.add), [("pe", pe_oa)])
            E("vector", lambda e: e.tensor_single_scalar(out=stat[:, 41:42], in_=ps[7][:, 64:65],
                                                         scalar=1e-6, op=ALU.add), [("pe", pe_ob)])
            FENCE("vector")
            E("vector", lambda e: e.reciprocal(out=stat[:, 42:44], in_=stat[:, 40:42]))
            FENCE("vector")
            E("vector", lambda e: e.tensor_scalar(out=agb[:, 0:64], in0=ps[6][:, 0:64],
                                                  scalar1=stat[:, 42:43], scalar2=None, op0=ALU.mult))
            v_aga = E("vector", lambda e: e.tensor_copy(out=agb[:, 64:128], in_=agb[:, 0:64]))
            E("vector", lambda e: e.tensor_scalar(out=agb[:, 128:192], in0=ps[7][:, 0:64],
                                                  scalar1=stat[:, 43:44], scalar2=None, op0=ALU.mult))
            v_agb2 = E("vector", lambda e: e.tensor_copy(out=agb[:, 192:256], in_=agb[:, 128:192]))
            bank_free[6] = ("ve", v_aga)
            bank_free[7] = ("ve", v_agb2)
            s4b, c4b = bank_free[4]
            pe_ta = E("tensor", lambda e: e.transpose(ps[4][:, 0:128], agb[:, 0:128], ident[:]),
                      [("ve", v_aga), (s4b, c4b)])
            s5b, c5b = bank_free[5]
            pe_tb = E("tensor", lambda e: e.transpose(ps[5][:, 0:128], agb[:, 128:256], ident[:]),
                      [("ve", v_agb2), (s5b, c5b)])
            E("vector", lambda e, gg=ga: e.tensor_copy(
                out=eo_v[0:64, gg],
                in_=ps[4][0:64, 0:128].rearrange("p (t c h) -> p t c h", t=8, c=8, h=2)[:, :, :, 0]),
              [("pe", pe_ta)])
            E("vector", lambda e, gg=ga: e.tensor_copy(
                out=eo_v[64:128, gg],
                in_=ps[4][64:128, 0:128].rearrange("p (t c h) -> p t c h", t=8, c=8, h=2)[:, :, :, 1]))
            E("vector", lambda e, gg=gb: e.tensor_copy(
                out=eo_v[0:64, gg],
                in_=ps[5][0:64, 0:128].rearrange("p (t c h) -> p t c h", t=8, c=8, h=2)[:, :, :, 0]),
              [("pe", pe_tb)])
            E("vector", lambda e, gg=gb: e.tensor_copy(
                out=eo_v[64:128, gg],
                in_=ps[5][64:128, 0:128].rearrange("p (t c h) -> p t c h", t=8, c=8, h=2)[:, :, :, 1]))
            bank_free[4] = ("ve", P.now("ve"))
            bank_free[5] = ("ve", P.now("ve"))
        attn_done = (P.now("ve"), P.now("se"))
        vtm_free = P.now("g")

        for tt in range(N_TILE):
            for og in range(2):
                sE, cE = bank_free[0]
                sO, cO = bank_free[1]
                for ct in range(8):
                    E("tensor", lambda e, c=ct, t=tt, o=og:
                      e.matmul(ps[0][:, 0:512],
                               eo_v[0:64, t * 16:(t + 1) * 16, :, c].rearrange("p g t -> p (g t)"),
                               wAB[0:64, (8 + c) * 1024 + o * 512: (8 + c) * 1024 + (o + 1) * 512],
                               start=(c == 0), stop=False),
                      ([("ve", attn_done[0]), ("se", attn_done[1]), ("w", w_o), (sE, cE), (sO, cO)]
                       if ct == 0 else ()))
                    E("tensor", lambda e, c=ct, t=tt, o=og:
                      e.matmul(ps[1][:, 0:512],
                               eo_v[64:128, t * 16:(t + 1) * 16, :, c].rearrange("p g t -> p (g t)"),
                               wAB[64:128, (8 + c) * 1024 + o * 512: (8 + c) * 1024 + (o + 1) * 512],
                               start=(c == 0), stop=(c == 7)))
                E("tensor", lambda e, o=og: e.matmul(ps[0][:, 0:512], brows[32:33, 1024:1152], brows[32:33, o * 512:(o + 1) * 512],
                                                     start=False, stop=True))
                pe_wo = P.now("pe")
                bconsume("wB", "pe", pe_wo)
                E("vector", lambda e: e.tensor_copy(out=big[:, 1664:2176], in_=ps[0][:, 0:512]), [("pe", pe_wo)])
                E("vector", lambda e: e.tensor_tensor(out=big[:, 1664:2176], in0=big[:, 1664:2176],
                                                      in1=ps[1][:, 0:512], op=ALU.add))
                vxa = E("vector", lambda e, t=tt, o=og: e.tensor_tensor(
                    out=u_a[:, t * 1024 + o * 512: t * 1024 + (o + 1) * 512], in0=big[:, 1664:2176],
                    in1=u_a[:, t * 1024 + o * 512: t * 1024 + (o + 1) * 512], op=ALU.add))
                bank_free[0] = ("ve", vxa)
                bank_free[1] = ("ve", vxa)

        for tt in range(N_TILE):
            key = "sc2pe" if tt == 0 else "scCpe"
            s2t = scr2 if tt == 0 else scrCv
            P.emit("vector", None, bwait(key))
            P.emit("scalar", None, bwait(key))
            emit_ln(u_a[:, tt * 1024:(tt + 1) * 1024], scr1, s2t)
            vh = E("vector", lambda e, s2=s2t: e.tensor_scalar(out=s2, in0=scr1, scalar1=stat[:, 4:5],
                                                               scalar2=None, op0=ALU.mult))
            for c in range(8):
                bk = 4 + (c % 4)
                s_, c_ = bank_free[bk]
                pe_t = E("tensor", lambda e, cc=c, b=bk, s2=s2t: e.transpose(ps[b][:, 0:128],
                                                                             s2[:, cc * 128:(cc + 1) * 128], ident[:]),
                         [("ve", vh), (s_, c_)])
                bconsume(key, "pe", pe_t)
                E("scalar", lambda e, cc=c, b=bk, t=tt: e.activation(
                    out=actT[:, cc * 256 + t * 128: cc * 256 + (t + 1) * 128], in_=ps[b][:, 0:128],
                    func=AFT.Identity, scale=g2c[:, cc:cc + 1], bias=b2c[:, cc:cc + 1]), [("pe", pe_t)])
                bank_free[bk] = ("se", P.now("se"))
        hT_done = (P.now("se"), P.now("ve"))

        for q in range(4):
            half = q % 2
            wait_half = bwait("wA" if half == 0 else "wB")
            for kc in range(8):
                DMA("sync", wAB[:, half * 8192 + kc * 1024: half * 8192 + (kc + 1) * 1024],
                    fcw_d[kc * 128:(kc + 1) * 128, q * 1024:(q + 1) * 1024], wait_half)
            w_fc = P.now("w")
            for m in range(8):
                M = q * 8 + m
                bk = m % 2
                s_, c_ = bank_free[bk]
                for kc in range(8):
                    E("tensor", lambda e, b=bk, k=kc, mm=m, h=half:
                      e.matmul(ps[b][:, 0:256],
                               wAB[:, h * 8192 + k * 1024 + mm * 128: h * 8192 + k * 1024 + (mm + 1) * 128],
                               actT[:, k * 256:(k + 1) * 256], start=(k == 0), stop=(k == 7)),
                      ([("w", w_fc), ("se", hT_done[0]), ("ve", hT_done[1]), (s_, c_)] if kc == 0 else ()))
                pe_fc = P.now("pe")
                bconsume("wA" if half == 0 else "wB", "pe", pe_fc)
                bconsume("actT", "pe", pe_fc)
                E("scalar", lambda e, b=bk, MM=M: e.activation(out=big[:, MM * 256:(MM + 1) * 256],
                                                               in_=ps[b][:, 0:256], func=AFT.Gelu,
                                                               bias=fcbc[:, MM:MM + 1]),
                  [("pe", pe_fc), ("ve", P.now("ve"))])
                bank_free[bk] = ("se", P.now("se"))
        g_done = P.now("se")

        for kc in range(32):
            slot = kc % 2
            DMA("sync", v_tm[:, slot * 1024:(slot + 1) * 1024], pjw_d[kc * 128:(kc + 1) * 128, :],
                ([("g", vtm_free)] if kc < 2 else []) + bwait("vslot%d" % slot))
            w_pj = P.now("w")
            for tt in range(N_TILE):
                for og in range(2):
                    bk = tt * 2 + og
                    if kc == 0:
                        s_, c_ = bank_free[bk]
                        wf = [("w", w_pj), ("se", g_done), (s_, c_)]
                    else:
                        wf = [("w", w_pj)]
                    E("tensor", lambda e, b=bk, k=kc, t=tt, o=og, sl=slot:
                      e.matmul(ps[b][:, 0:512], big[:, k * 256 + t * 128: k * 256 + (t + 1) * 128],
                               v_tm[:, sl * 1024 + o * 512: sl * 1024 + (o + 1) * 512],
                               start=(k == 0), stop=False), wf)
                    bconsume("vslot%d" % slot, "pe", P.now("pe"))
        for tt in range(N_TILE):
            for og in range(2):
                bk = tt * 2 + og
                E("tensor", lambda e, b=bk, o=og: e.matmul(ps[b][:, 0:512], brows[64:65, 1024:1152],
                                                           brows[64:65, o * 512:(o + 1) * 512], start=False, stop=True))
        pe_pj = P.now("pe")

        for tt in range(N_TILE):
            r0 = tok0 + tt * 128
            lx = DMA("gpsimd", scr0, x_d[r0:r0 + 128, :],
                     bwait("s0pe") + [("ve", P.now("ve")), ("o", P.now("o"))])
            P.emit("vector", None, bwait("sc2pe"))
            for og in range(2):
                bk = tt * 2 + og
                E("vector", lambda e, b=bk, t=tt, o=og: e.tensor_tensor(
                    out=scr1[:, o * 512:(o + 1) * 512], in0=ps[b][:, 0:512],
                    in1=u_a[:, t * 1024 + o * 512: t * 1024 + (o + 1) * 512], op=ALU.add), [("pe", pe_pj)])
                bank_free[bk] = ("ve", P.now("ve"))
            emit_ln(scr1, scr2, scr3)
            E("vector", lambda e, t=tt: e.scalar_tensor_tensor(
                out=x1_a[:, t * 1024:(t + 1) * 1024], in0=scr2, scalar=stat[:, 4:5], in1=scr0,
                op0=ALU.mult, op1=ALU.add), [("g", lx)])
            FENCE("vector")
            emit_ln(x1_a[:, tt * 1024:(tt + 1) * 1024], scr2, scr3)
            E("vector", lambda e: e.reduce_max(out=stat[:, 5:6], in_=scr2, axis=AX, apply_absolute_value=True))
            FENCE("vector")
            E("vector", lambda e: e.tensor_tensor(out=stat[:, 6:7], in0=stat[:, 5:6], in1=stat[:, 4:5], op=ALU.mult))
            FENCE("vector")
            E("vector", lambda e: e.tensor_single_scalar(out=stat[:, 7:8], in_=stat[:, 6:7], scalar=1e-5, op=ALU.max))
            FENCE("vector")
            E("vector", lambda e: e.reciprocal(out=stat[:, 8:9], in_=stat[:, 7:8]))
            FENCE("vector")
            E("vector", lambda e: e.tensor_single_scalar(out=stat[:, 9:10], in_=stat[:, 8:9], scalar=127.0,
                                                         op=ALU.mult))
            FENCE("vector")
            E("vector", lambda e: e.tensor_tensor(out=stat[:, 10:11], in0=stat[:, 9:10], in1=stat[:, 4:5],
                                                  op=ALU.mult))
            FENCE("vector")
            E("vector", lambda e, t=tt: e.tensor_copy(out=stat[:, 20 + t:21 + t], in_=stat[:, 7:8]))
            FENCE("vector")
            E("vector", lambda e: e.tensor_scalar(out=scr1, in0=scr2, scalar1=stat[:, 10:11],
                                                  scalar2=128.5, op0=ALU.mult, op1=ALU.add))
            emit_ladder(scr1, scr3, scr0)
            vkx = E("vector", lambda e: e.scalar_tensor_tensor(out=scr0, in0=scr1, scalar=128.0, in1=scr3,
                                                               op0=ALU.subtract, op1=ALU.subtract))
            for c in range(8):
                bk = 4 + (c % 4)
                s_, c_ = bank_free[bk]
                pe_t = E("tensor", lambda e, cc=c, b=bk: e.transpose(ps[b][:, 0:128],
                                                                     scr0[:, cc * 128:(cc + 1) * 128], ident[:]),
                         [("ve", vkx), (s_, c_)])
                bconsume("s0pe", "pe", pe_t)
                E("scalar", lambda e, cc=c, b=bk, t=tt: e.activation(
                    out=kxT[:, cc * 256 + t * 128: cc * 256 + (t + 1) * 128], in_=ps[b][:, 0:128],
                    func=AFT.Identity), [("pe", pe_t)])
                bank_free[bk] = ("se", P.now("se"))
        kxT_done = (P.now("se"), P.now("ve"))

        if p == 0:
            def g0_scan(w_src, ntile, tile_w):
                for kc in range(ntile):
                    half = kc % 2
                    lw = DMA("sync", wAB[:, half * 4096: half * 4096 + tile_w],
                             w_src[kc * 128:(kc + 1) * 128, 0:tile_w],
                             bwait("wA", "g0m%d" % half))
                    v_r = E("vector", lambda e, k=kc, h=half, twd=tile_w: e.reduce_sum(
                        out=stat[:, 24 + k:25 + k], in_=wAB[:, h * 4096: h * 4096 + twd], axis=AX,
                        apply_absolute_value=True), [("w", lw)])
                    bconsume("g0m%d" % half, "ve", v_r)
                    bconsume("wA", "ve", v_r)

            def g0_stats(ntile, c_m, c_th):
                FENCE("vector")
                E("vector", lambda e, n=ntile: e.reduce_sum(out=stat[:, 60:61], in_=stat[:, 24:24 + n], axis=AX))
                v_acc = P.now("ve")
                FENCE("vector")
                s6_, c6_ = bank_free[6]
                pe_r1 = E("tensor", lambda e: e.matmul(ps[6][0:1, 0:1], onecol[:], stat[:, 60:61],
                                                       start=True, stop=True), [("ve", v_acc), (s6_, c6_)])
                E("vector", lambda e: e.tensor_copy(out=stat[0:1, 62:63], in_=ps[6][0:1, 0:1]),
                  [("pe", pe_r1)])
                FENCE("vector")
                s7_, c7_ = bank_free[7]
                pe_r2 = E("tensor", lambda e: e.matmul(ps[7][:, 0:1], brows[0:1, 1024:1152],
                                                       stat[0:1, 62:63], start=True, stop=True),
                          [("ve", P.now("ve")), (s7_, c7_)])
                E("vector", lambda e: e.tensor_copy(out=wsc[:, 6:7], in_=ps[7][:, 0:1]), [("pe", pe_r2)])
                bank_free[6] = ("ve", P.now("ve"))
                bank_free[7] = ("ve", P.now("ve"))
                FENCE("vector")
                E("vector", lambda e, cm=c_m: e.tensor_scalar(out=wsc[:, cm:cm + 1], in0=wsc[:, 6:7],
                                                              scalar1=1.0 / (1024.0 * 4096.0), scalar2=1e-5,
                                                              op0=ALU.mult, op1=ALU.max))
                FENCE("vector")
                E("vector", lambda e, cm=c_m, ct=c_th: e.tensor_single_scalar(
                    out=wsc[:, ct:ct + 1], in_=wsc[:, cm:cm + 1], scalar=0.5, op=ALU.mult))
                FENCE("vector")

            def g0_quant(w_src, q_dst, ntile, tile_w, c_th):
                for kc in range(ntile):
                    half = kc % 2
                    base = 8192 + half * 4096
                    lw = DMA("sync", wAB[:, base: base + tile_w],
                             w_src[kc * 128:(kc + 1) * 128, 0:tile_w],
                             bwait("wB", "g0q%d" % half))
                    s_sgn = E("scalar", lambda e, b0=base, twd=tile_w: e.activation(
                        out=big[:, 4096:4096 + twd], in_=wAB[:, b0: b0 + twd],
                        func=AFT.Sign), [("w", lw), ("pe", P.now("pe")), ("ve", P.now("ve")), ("g", P.now("g"))])
                    s_abs = E("scalar", lambda e, b0=base, twd=tile_w: e.activation(
                        out=big[:, 0:twd], in_=wAB[:, b0: b0 + twd], func=AFT.Abs),
                      [("w", lw), ("pe", P.now("pe")), ("ve", P.now("ve")), ("g", P.now("g"))])
                    bconsume("g0q%d" % half, "se", s_abs)
                    bconsume("wB", "se", s_abs)
                    E("vector", lambda e, ct=c_th, twd=tile_w: e.tensor_single_scalar(
                        out=big[:, 0:twd], in_=big[:, 0:twd], scalar=wsc[:, ct:ct + 1], op=ALU.is_gt),
                      [("se", s_abs), ("g", P.now("g"))])
                    v_q = E("vector", lambda e, twd=tile_w: e.tensor_tensor(
                        out=k2T[:, 0:twd], in0=big[:, 0:twd],
                        in1=big[:, 4096:4096 + twd], op=ALU.mult), [("se", s_sgn), ("g", P.now("g"))])
                    DMA("gpsimd", q_dst[kc * 128:(kc + 1) * 128, 0:tile_w],
                        k2T[:, 0:tile_w], [("ve", v_q)])

            g0_scan(wgT_d, 8, 4096)
            g0_stats(8, 0, 3)
            g0_quant(wgT_d, qg_d, 8, 4096, 3)
            g0_scan(wuT_d, 8, 4096)
            g0_stats(8, 1, 4)
            g0_quant(wuT_d, qu_d, 8, 4096, 4)
            g0_scan(wdT_d, 32, 1024)
            g0_stats(32, 2, 5)
            g0_quant(wdT_d, qd_d, 32, 1024, 5)
            qw_done = P.now("g")
        else:
            qw_done = P.now("g")

        for tt in range(N_TILE):
            cb = 12 + tt * 4
            E("vector", lambda e, c=cb, t=tt: e.tensor_tensor(out=stat[:, c:c + 1], in0=stat[:, 20 + t:21 + t],
                                                              in1=wsc[:, 0:1], op=ALU.mult))
            FENCE("vector")
            E("vector", lambda e, c=cb: e.tensor_single_scalar(out=stat[:, c:c + 1], in_=stat[:, c:c + 1],
                                                               scalar=1.0 / 127.0, op=ALU.mult))
            FENCE("vector")
            E("vector", lambda e, c=cb, t=tt: e.tensor_tensor(out=stat[:, c + 1:c + 2], in0=stat[:, 20 + t:21 + t],
                                                              in1=wsc[:, 1:2], op=ALU.mult))
            FENCE("vector")
            E("vector", lambda e, c=cb: e.tensor_single_scalar(out=stat[:, c + 1:c + 2], in_=stat[:, c + 1:c + 2],
                                                               scalar=1.0 / 127.0, op=ALU.mult))
            FENCE("vector")
        for e8 in range(16):
            src = qg_d if e8 < 8 else qu_d
            col0 = (e8 % 8) * 512
            slot = e8 % 2
            lwaits = bwait("k2s%d" % slot) + ([("g", qw_done)] if e8 == 0 else [])
            for kc in range(8):
                DMA("sync", k2T[:, slot * 4096 + kc * 512: slot * 4096 + (kc + 1) * 512],
                    src[kc * 128:(kc + 1) * 128, col0:col0 + 512], lwaits)
            w_e = P.now("w")
            for tt in range(N_TILE):
                cb = 12 + tt * 4
                og = e8 % 8
                bk = (e8 * 2 + tt) % 4
                s_, c_ = bank_free[bk]
                for kc in range(8):
                    E("tensor", lambda e, b=bk, k=kc, t=tt, sl=slot:
                      e.matmul(ps[b][:, 0:512], kxT[:, k * 256 + t * 128: k * 256 + (t + 1) * 128],
                               k2T[:, sl * 4096 + k * 512: sl * 4096 + (k + 1) * 512],
                               start=(k == 0), stop=(k == 7)),
                      ([("w", w_e), ("se", kxT_done[0]), ("ve", kxT_done[1]), (s_, c_)] if kc == 0 else ()))
                pe_mm = P.now("pe")
                bconsume("k2s%d" % slot, "pe", pe_mm)
                if e8 < 8:
                    E("scalar", lambda e, b=bk, t=tt, o=og, c=cb: e.activation(
                        out=big[:, t * 4096 + o * 512: t * 4096 + (o + 1) * 512], in_=ps[b][:, 0:512],
                        func=AFT.Silu, scale=stat[:, c:c + 1]), [("pe", pe_mm), ("ve", P.now("ve"))])
                    bank_free[bk] = ("se", P.now("se"))
                else:
                    E("vector", lambda e, b=bk, t=tt, o=og, c=cb: e.scalar_tensor_tensor(
                        out=big[:, t * 4096 + o * 512: t * 4096 + (o + 1) * 512], in0=ps[b][:, 0:512],
                        scalar=stat[:, c + 1:c + 2], in1=big[:, t * 4096 + o * 512: t * 4096 + (o + 1) * 512],
                        op0=ALU.mult, op1=ALU.mult), [("pe", pe_mm), ("se", P.now("se"))])
                    E("vector", lambda e, t=tt, o=og: e.reduce_max(
                        out=stat[:, 24 + t * 8 + o:25 + t * 8 + o], in_=big[:, t * 4096 + o * 512: t * 4096 + (o + 1) * 512],
                        axis=AX, apply_absolute_value=True))
                    bank_free[bk] = ("ve", P.now("ve"))
        bconsume("qdb1", "pe", P.now("pe"))
        for tt in range(N_TILE):
            cb2 = 56 + tt
            if True:
                if True:
                    FENCE("vector")
                    E("vector", lambda e, t=tt: e.reduce_max(out=stat[:, 52:53], in_=stat[:, 24 + t * 8:32 + t * 8], axis=AX))
                    FENCE("vector")
                    E("vector", lambda e: e.tensor_single_scalar(out=stat[:, 53:54], in_=stat[:, 52:53],
                                                                 scalar=1e-5, op=ALU.max))
                    FENCE("vector")
                    E("vector", lambda e: e.reciprocal(out=stat[:, 54:55], in_=stat[:, 53:54]))
                    FENCE("vector")
                    E("vector", lambda e: e.tensor_single_scalar(out=stat[:, 55:56], in_=stat[:, 54:55],
                                                                 scalar=127.0, op=ALU.mult))
                    FENCE("vector")
                    cb2 = 56 + tt
                    E("vector", lambda e, c=cb2: e.tensor_tensor(out=stat[:, c:c + 1], in0=stat[:, 53:54],
                                                                 in1=wsc[:, 2:3], op=ALU.mult))
                    FENCE("vector")
                    E("vector", lambda e, c=cb2: e.tensor_single_scalar(out=stat[:, c:c + 1],
                                                                        in_=stat[:, c:c + 1],
                                                                        scalar=1.0 / 127.0, op=ALU.mult))
                    FENCE("vector")
                    for c4 in range(4):
                        odd = c4 % 2
                        yb = actT[:, 0:1024] if odd else scr1
                        rb = actT[:, 1024:2048] if odd else scr3
                        resb = scrCv if odd else scr0
                        key = "scCpe" if odd else "s0pe"
                        P.emit("vector", None, bwait(key) + (bwait("actT") if odd else []))
                        E("vector", lambda e, t=tt, c=c4, y=yb: e.tensor_scalar(
                            out=y, in0=big[:, t * 4096 + c * 1024: t * 4096 + (c + 1) * 1024],
                            scalar1=stat[:, 55:56], scalar2=128.5, op0=ALU.mult, op1=ALU.add))
                        emit_ladder(yb, rb, resb)
                        vk2 = E("vector", lambda e, y=yb, r=rb, res=resb: e.scalar_tensor_tensor(
                            out=res, in0=y, scalar=128.0, in1=r,
                            op0=ALU.subtract, op1=ALU.subtract))
                        for j in range(8):
                            kc2 = c4 * 8 + j
                            bk2 = 4 + (j % 4)
                            s2, c2_ = bank_free[bk2]
                            pe_t = E("tensor", lambda e, jj=j, b=bk2, res=resb: e.transpose(
                                ps[b][:, 0:128], res[:, jj * 128:(jj + 1) * 128], ident[:]),
                                [("ve", vk2), (s2, c2_)])
                            bconsume(key, "pe", pe_t)
                            E("scalar", lambda e, k=kc2, b=bk2, t=tt: e.activation(
                                out=k2T[:, k * 256 + t * 128: k * 256 + (t + 1) * 128], in_=ps[b][:, 0:128],
                                func=AFT.Identity), [("pe", pe_t), ("g", qw_done)])
                            bank_free[bk2] = ("se", P.now("se"))
        k2_done = (P.now("se"), P.now("ve"))

        for kc in range(32):
            for og in range(2):
                sap = qdb[:] if og == 0 else kxT[:, 0:512]
                DMA("sync", sap, qd_d[kc * 128:(kc + 1) * 128, og * 512:(og + 1) * 512],
                    bwait("qdb%d" % og))
                w_qd = P.now("w")
                for tt in range(N_TILE):
                    bk = tt * 2 + og
                    if kc == 0:
                        s_, c_ = bank_free[bk]
                        wf = [("w", w_qd), ("se", k2_done[0]), ("ve", k2_done[1]), (s_, c_)]
                    else:
                        wf = [("w", w_qd)]
                    E("tensor", lambda e, b=bk, k=kc, t=tt, sa=sap:
                      e.matmul(ps[b][:, 0:512], k2T[:, k * 256 + t * 128: k * 256 + (t + 1) * 128],
                               sa,
                               start=(k == 0), stop=(k == 31)), wf)
                    bconsume("qdb%d" % og, "pe", P.now("pe"))
        pe_dn = P.now("pe")
        bconsume("k2s0", "pe", pe_dn)
        bconsume("k2s1", "pe", pe_dn)
        for tt in range(N_TILE):
            cb2 = 56 + tt
            GUARD("vector", "pe")
            GUARD("scalar", "pe")
            for og in range(2):
                bk = tt * 2 + og
                E("vector", lambda e, b=bk, o=og, c=cb2: e.tensor_scalar(
                    out=scr1[:, o * 512:(o + 1) * 512], in0=ps[b][:, 0:512], scalar1=stat[:, c:c + 1],
                    scalar2=None, op0=ALU.mult), [("pe", pe_dn)])
                bank_free[bk] = ("ve", P.now("ve"))
            emit_ln(scr1, scr2, scr3)
            v_out = E("vector", lambda e, t=tt: e.scalar_tensor_tensor(
                out=scr0, in0=scr2, scalar=stat[:, 4:5], in1=x1_a[:, t * 1024:(t + 1) * 1024],
                op0=ALU.mult, op1=ALU.add))
            r0 = tok0 + tt * 128
            DMA("gpsimd", out_d[r0:r0 + 128, :], scr0, [("ve", v_out)], osem="o")
            out_total[0] += 16

    P.emit("vector", None, [("o", out_total[0])])

    with ExitStack() as st:
        sems = {s: st.enter_context(nc.semaphore(s)) for s in SEMS}
        block = st.enter_context(nc.Block())

        def mk(name):
            def run(eng):
                for fn, ws, inc in P.ops[name]:
                    for s, v in ws:
                        eng.wait_ge(sems[s], v)
                    ins = fn(eng) if fn is not None else None
                    if inc is not None:
                        if ins is None:
                            ins = eng.engine_nop()
                        ins.then_inc(sems[inc[0]], inc[1])
            return run

        block.sync(mk("sync"))
        block.gpsimd(mk("gpsimd"))
        block.tensor(mk("tensor"))
        block.vector(mk("vector"))
        block.scalar(mk("scalar"))
    return nc


def _mk_brows(bv, bo, pjb):
    m = np.zeros((65, D + 128), dtype=np.float32)
    for r, b in ((0, bv), (32, bo), (64, pjb)):
        m[r, 0:D] = b
        m[r, D:D + 128] = 1.0
    return m


def kernel(**inputs):
    def A(name):
        return np.ascontiguousarray(np.asarray(inputs[name], dtype=np.float32))

    x = A("L0_attn_q_x").reshape(16384, D)
    consts = {
        "wq": A("L0_attn_wq"), "wk": A("L0_attn_wk"), "wv": A("L0_attn_wv"), "wo": A("L0_attn_wo"),
        "fcw": A("L0_attn_fc_w"), "pjw": A("L0_attn_proj_w"),
        "wgT": np.ascontiguousarray(A("L0_ffn_w_gate").T),
        "wuT": np.ascontiguousarray(A("L0_ffn_w_up").T),
        "wdT": np.ascontiguousarray(A("L0_ffn_w_down").T),
        "ident": np.eye(128, dtype=np.float32),
        "mask": np.kron(np.eye(8, dtype=np.float32), np.ones((16, 16), dtype=np.float32)),
        "brows": _mk_brows(A("L0_attn_bv"), A("L0_attn_bo"), A("L0_attn_proj_b")),
        "bqc": np.ascontiguousarray(A("L0_attn_bq").reshape(8, 128).T),
        "bkc": np.ascontiguousarray(A("L0_attn_bk").reshape(8, 128).T),
        "g1c": np.ascontiguousarray(A("L0_attn_ln1_w").reshape(8, 128).T),
        "b1c": np.ascontiguousarray(A("L0_attn_ln1_b").reshape(8, 128).T),
        "g2c": np.ascontiguousarray(A("L0_attn_ln2_w").reshape(8, 128).T),
        "b2c": np.ascontiguousarray(A("L0_attn_ln2_b").reshape(8, 128).T),
        "fcbc": np.ascontiguousarray(A("L0_attn_fc_b").reshape(32, 128).T),
    }
    nc = build()
    in_maps = []
    for c in range(NCORE):
        m = dict(consts)
        m["x"] = np.ascontiguousarray(x[c * TOK_CORE:(c + 1) * TOK_CORE])
        in_maps.append(m)
    res = run_bass_kernel_spmd(nc, in_maps, core_ids=list(range(NCORE)))
    out = np.concatenate([np.asarray(r["out"]) for r in res.results], axis=0)
    return out.reshape(2, 8192, D).astype(np.float32)
